# version 14
# baseline (speedup 1.0000x reference)
import numpy as np
from contextlib import ExitStack

import concourse.bass as bass
import concourse.mybir as mybir
from concourse.bass_utils import run_bass_kernel_spmd

F32 = mybir.dt.float32
BF16 = mybir.dt.bfloat16
I32 = mybir.dt.int32
AF = mybir.ActivationFunctionType
ALU = mybir.AluOpType
AX = mybir.AxisListType

S = 16384
Q = 4096
D = 1024
H = 16
DH = 64
HP = 8
I_FFN = 2816
NIC = 22
NT1 = S // 512
NT1B = Q // 512
NTC = Q // 128
NQT = Q // 512
NPAIR = S // 256
EPS = 1e-5


def set_dims(s, q):
    global S, Q, NT1, NT1B, NTC, NQT, NPAIR
    S, Q = s, q
    NT1 = S // 512
    NT1B = Q // 512
    NTC = Q // 128
    NQT = Q // 512
    NPAIR = S // 256

DBG = dict(p1=False, p1x=False, p2=False, phases=("p0", "p1", "p1b", "p2", "p3"))

ENGINES = ("tensor", "vector", "scalar", "sync", "gpsimd")
DMA_SEM_OF = {"sync": "dS", "gpsimd": "dG", "scalar": "dA"}
ENG_SEM_OF = {"tensor": "cT", "vector": "cV", "scalar": "cA", "gpsimd": "cG", "sync": "cS"}


class Sched:

    def __init__(self, nc, sems, ctrs):
        self.nc = nc
        self.sems = sems
        self.ctrs = ctrs
        self.ops = {e: [] for e in ENGINES}

    def add(self, eng, fn, deps=(), dma=False, n=1):
        semname = DMA_SEM_OF[eng] if dma else ENG_SEM_OF[eng]
        ctr = self.ctrs[semname]
        ctr.v += 16 * n if dma else 1
        ev = (semname, ctr.v)
        deps = [d for d in deps if d is not None]
        self.ops[eng].append((fn, deps, semname, ctr.v, dma))
        return ev

    def dma(self, eng, fn, deps=(), n=1):
        return self.add(eng, fn, deps, dma=True, n=n)

    def emit(self, blk):
        for eng in ENGINES:
            ops = self.ops[eng]
            if not ops:
                continue

            def body(e, ops=ops, eng=eng):
                waited = {}
                own = ENG_SEM_OF[eng]
                for fn, deps, semname, tgt, dma in ops:
                    for (dsem, dtgt) in deps:
                        if dsem == own:
                            continue
                        if waited.get(dsem, -1) >= dtgt:
                            continue
                        e.wait_ge(self.sems[dsem], dtgt)
                        waited[dsem] = dtgt
                    last = fn(e)
                    assert last is not None, f"op on {eng} returned None"
                    last.then_inc(self.sems[semname], 16 if dma else 1)
            getattr(blk, eng)(body)
        self.ops = {e: [] for e in ENGINES}


class _Ctr:
    __slots__ = ("v",)
    def __init__(self):
        self.v = 0


def _quant_emit(vector, x_f32, q_bf16, tu, t1, tg, tg2, bm, sc, w):
    nblk = w // 16
    x3 = x_f32.rearrange("p (n b) -> p n b", b=16)
    D_ = vector.drain
    vector.tensor_reduce(bm.unsqueeze(2), x3, AX.X, ALU.max, apply_absolute_value=True)
    vector.tensor_single_scalar(tu.bitcast(I32), x_f32.bitcast(I32), 0x7FFFFFFF, ALU.bitwise_and)
    D_()
    vector.tensor_scalar(sc, bm, float(np.float32(1.0) / np.float32(6.0)), 1e-12, ALU.mult, ALU.max)
    D_()
    vector.reciprocal(bm, sc)
    D_()
    rb = bm.unsqueeze(2).broadcast_to([128, nblk, 16])
    u3 = tu.rearrange("p (n b) -> p n b", b=16)
    vector.tensor_tensor(u3, u3, rb, ALU.mult)
    D_()
    vector.tensor_scalar_mul(t1, tu, float(1 << 22))
    D_()
    vector.tensor_add(tg, t1, tu)
    D_()
    vector.tensor_sub(tg, tg, t1)
    D_()
    vector.tensor_scalar_max(tg, tg, 1.0)
    vector.tensor_single_scalar(t1, tu, 0.75, ALU.is_le)
    D_()
    vector.tensor_scalar(tu, tu, 0.25, 0.5, ALU.is_gt, ALU.mult)
    D_()
    vector.select(tg2, t1.bitcast(I32), tu, tg)
    D_()
    sb3 = sc.unsqueeze(2).broadcast_to([128, nblk, 16])
    g3 = tg2.rearrange("p (n b) -> p n b", b=16)
    vector.tensor_tensor(g3, g3, sb3, ALU.mult)
    vector.tensor_single_scalar(t1.bitcast(I32), x_f32.bitcast(I32), -0x80000000, ALU.bitwise_and)
    D_()
    vector.tensor_tensor(tu.bitcast(I32), tg2.bitcast(I32), t1.bitcast(I32), ALU.bitwise_or)
    D_()
    return vector.tensor_copy(q_bf16, tu)




def _quant_emit2(vector, jobs):
    if len(jobs) == 1:
        x, q, T, w = jobs[0]
        return _quant_emit(vector, x, q, T["tu"][:, 0:w], T["t1"][:, 0:w],
                           T["tg"][:, 0:w], T["tg2"][:, 0:w],
                           T["bm"][:, 0:w // 16], T["sc"][:, 0:w // 16], w)
    steps = []
    for x_f32, q_bf16, T, w in jobs:
        nblk = w // 16
        tu, t1, tg, tg2 = (T[k][:, 0:w] for k in ("tu", "t1", "tg", "tg2"))
        bm, sc = T["bm"][:, 0:nblk], T["sc"][:, 0:nblk]
        x3 = x_f32.rearrange("p (n b) -> p n b", b=16)
        rb = bm.unsqueeze(2).broadcast_to([128, nblk, 16])
        u3 = tu.rearrange("p (n b) -> p n b", b=16)
        sb3 = sc.unsqueeze(2).broadcast_to([128, nblk, 16])
        g3 = tg2.rearrange("p (n b) -> p n b", b=16)
        c16 = float(np.float32(1.0) / np.float32(6.0))
        steps.append([
            lambda v, a=bm, b=x3: v.tensor_reduce(a.unsqueeze(2), b, AX.X, ALU.max,
                                                  apply_absolute_value=True),
            lambda v, a=tu, b=x_f32: v.tensor_single_scalar(
                a.bitcast(I32), b.bitcast(I32), 0x7FFFFFFF, ALU.bitwise_and),
            lambda v, a=sc, b=bm: v.tensor_scalar(a, b, c16, 1e-12, ALU.mult, ALU.max),
            lambda v, a=bm, b=sc: v.reciprocal(a, b),
            lambda v, a=u3, b=rb: v.tensor_tensor(a, a, b, ALU.mult),
            lambda v, a=t1, b=tu: v.tensor_scalar_mul(a, b, float(1 << 22)),
            lambda v, a=tg, b=t1, c=tu: v.tensor_add(a, b, c),
            lambda v, a=tg, b=t1: v.tensor_sub(a, a, b),
            lambda v, a=tg: v.tensor_scalar_max(a, a, 1.0),
            lambda v, a=t1, b=tu: v.tensor_single_scalar(a, b, 0.75, ALU.is_le),
            lambda v, a=tu: v.tensor_scalar(a, a, 0.25, 0.5, ALU.is_gt, ALU.mult),
            lambda v, a=tg2, b=t1, c=tu, d=tg: v.select(a, b.bitcast(I32), c, d),
            lambda v, a=g3, b=sb3: v.tensor_tensor(a, a, b, ALU.mult),
            lambda v, a=t1, b=x_f32: v.tensor_single_scalar(
                a.bitcast(I32), b.bitcast(I32), -0x80000000, ALU.bitwise_and),
            lambda v, a=tu, b=tg2, c=t1: v.tensor_tensor(
                a.bitcast(I32), b.bitcast(I32), c.bitcast(I32), ALU.bitwise_or),
            lambda v, a=q_bf16, b=tu: v.tensor_copy(a, b),
        ])
    r = None
    for k in range(len(steps[0])):
        for s in steps:
            r = s[k](vector)
    return r


def build_program():
    nc = bass.Bass("TRN2", target_bir_lowering=False)

    xb = nc.dram_tensor("xb", [S, D], F32, kind="ExternalInput")
    xq = nc.dram_tensor("xq", [Q, D], F32, kind="ExternalInput")
    wq = nc.dram_tensor("wq", [D, D], F32, kind="ExternalInput")
    wk = nc.dram_tensor("wk", [D, D], F32, kind="ExternalInput")
    wv = nc.dram_tensor("wv", [D, D], F32, kind="ExternalInput")
    wo = nc.dram_tensor("wo", [D, D], F32, kind="ExternalInput")
    wg = nc.dram_tensor("wg", [I_FFN, D], F32, kind="ExternalInput")
    wu = nc.dram_tensor("wu", [I_FFN, D], F32, kind="ExternalInput")
    wd = nc.dram_tensor("wd", [D, I_FFN], F32, kind="ExternalInput")
    identb = nc.dram_tensor("identb", [128, 128], BF16, kind="ExternalInput")
    out = nc.dram_tensor("out", [Q, D], F32, kind="ExternalOutput")

    kT_scr = nc.dram_tensor("kT_scr", [HP, 128, S], BF16)
    v_scr = nc.dram_tensor("v_scr", [H, S, DH], BF16)
    qT_scr = nc.dram_tensor("qT_scr", [HP, 128, Q], BF16)
    xnqT_scr = nc.dram_tensor("xnqT_scr", [NTC, 128, D], BF16)
    oT_scr = nc.dram_tensor("oT_scr", [HP, 128, Q], BF16)
    wgT_scr = nc.dram_tensor("wgT_scr", [8, 128, I_FFN], BF16)
    wuT_scr = nc.dram_tensor("wuT_scr", [8, 128, I_FFN], BF16)
    wdT_scr = nc.dram_tensor("wdT_scr", [NIC, 128, D], BF16)
    hqT_scr = nc.dram_tensor("hqT_scr", [NTC, 128, NIC * 128], BF16)

    xnq_scr = None
    xnf_scr = None
    dbg = {}
    if DBG["p1"]:
        for nm, ref in (("d_kT", kT_scr), ("d_v", v_scr), ("d_qT", qT_scr),
                        ("d_xnqT", xnqT_scr), ("d_wgT", wgT_scr), ("d_wuT", wuT_scr),
                        ("d_wdT", wdT_scr)):
            dbg[nm] = (nc.dram_tensor(nm, list(ref.shape), BF16, kind="ExternalOutput"), ref)
    if DBG["p2"]:
        dbg["d_oT"] = (nc.dram_tensor("d_oT", [HP, 128, Q], BF16, kind="ExternalOutput"), oT_scr)
        nc._p2_oraw = nc.dram_tensor("d_oraw", [128, Q], F32, kind="ExternalOutput")
        nc._p2_l = nc.dram_tensor("d_l", [64, Q], F32, kind="ExternalOutput")
        nc._p2_p = nc.dram_tensor("d_p", [128, 2, 2, 1024], BF16, kind="ExternalOutput")
        nc._p2_rlb = nc.dram_tensor("d_rlb", [128, Q], F32, kind="ExternalOutput")
        nc._p2_vp = nc.dram_tensor("d_vp", [128, 2, 2 * (S // 256) * 65], BF16, kind="ExternalOutput")
        nc._p2_ev = nc.dram_tensor("d_ev", [64, 512], F32, kind="ExternalOutput")

    with ExitStack() as gst:
        sems = {nm: gst.enter_context(nc.semaphore(nm))
                for nm in ("dS", "dG", "dA", "cT", "cA", "cV", "cG", "cS")}
        ctrs = {nm: _Ctr() for nm in sems}
        sch = Sched(nc, sems, ctrs)
        id_sb = gst.enter_context(nc.sbuf_tensor("id_sb", [128, 128], BF16))

        with nc.Block("ph_boot") as blk:
            ev_id = sch.dma("sync", lambda e: e.dma_start(out=id_sb[:], in_=identb[:, :]))
            sch.emit(blk)

        ev_p1 = None
        if "p1" in DBG["phases"]:
            p0args = (wg, wu, wd, wgT_scr, wuT_scr, wdT_scr) if "p0" in DBG["phases"] else None
            emit_p1(nc, sch, id_sb, ev_id, xb, wq, wk, wv,
                    kT_scr, v_scr, None, None, NT1, p0args=p0args)
        if "p1b" in DBG["phases"]:
            emit_p1(nc, sch, id_sb, ev_id, xq, wq, wk, wv,
                    None, None, qT_scr, xnqT_scr, NT1B, xnq_scr=xnq_scr, xnf_scr=xnf_scr)
        if "p3" in DBG["phases"]:
            emit_p3f(nc, sch, id_sb, ev_id, wgT_scr, wuT_scr, xnqT_scr, hqT_scr)
        if "p2" in DBG["phases"]:
            emit_p2(nc, sch, kT_scr, v_scr, qT_scr, oT_scr)
        if "p3" in DBG["phases"]:
            emit_p3t(nc, sch, xq, wo, wdT_scr, hqT_scr, oT_scr, out)

        if dbg:
            with nc.Block("ph_dbg") as blk:
                for nm, (dtensor, src) in dbg.items():
                    sch.dma("sync", lambda e, d=dtensor, s=src: e.dma_start(out=d[...], in_=s[...]))
                sch.emit(blk)

        with nc.Block("ph_fin") as blk:
            @blk.sync
            def _(sync):
                sync.wait_ge(sems["dS"], ctrs["dS"].v)
                sync.wait_ge(sems["dG"], ctrs["dG"].v)
    return nc


def make_p0_recorder(nc, sch, st, id_sb, ev_id, wg, wu, wd, wgT_scr, wuT_scr, wdT_scr):
    jobs = [(wg, wgT_scr, d, NIC) for d in range(8)] + \
           [(wu, wuT_scr, d, NIC) for d in range(8)] + \
           [(wd, wdT_scr, i, 8) for i in range(NIC)]
    cb = st.enter_context(nc.sbuf_tensor("p0_cb", [128, NIC * 128], F32))
    qb = st.enter_context(nc.sbuf_tensor("p0_qb", [128, NIC * 128], BF16))
    p0_T = []
    for k in range(2):
        p0_T.append(dict(
            tu=st.enter_context(nc.sbuf_tensor(f"p0_tu{k}", [128, 704], F32)),
            t1=st.enter_context(nc.sbuf_tensor(f"p0_t1{k}", [128, 704], F32)),
            tg=st.enter_context(nc.sbuf_tensor(f"p0_tg{k}", [128, 704], F32)),
            tg2=st.enter_context(nc.sbuf_tensor(f"p0_tg2{k}", [128, 704], F32)),
            bm=st.enter_context(nc.sbuf_tensor(f"p0_bm{k}", [128, 44], F32)),
            sc=st.enter_context(nc.sbuf_tensor(f"p0_sc{k}", [128, 44], F32))))
    wrow = st.enter_context(nc.sbuf_tensor("p0_wrow", [128, I_FFN], BF16))
    trp = st.enter_context(nc.psum_tensor("p0_trps", [128, 512], F32))

    state = dict(ev_load=[None] * len(jobs), ev_quant=[None] * len(jobs),
                 ev_tr=[[] for _ in jobs], ev_ev=[[] for _ in jobs],
                 ev_store=[None] * len(jobs))

    def record_job(j):
        if j >= len(jobs):
            return
        wsrc, wdst, d, nt = jobs[j]
        w = nt * 128
        ev_load, ev_quant = state["ev_load"], state["ev_quant"]
        ev_tr, ev_ev, ev_store = state["ev_tr"], state["ev_ev"], state["ev_store"]
        ev_load[j] = sch.dma(
            "gpsimd",
            lambda e, wsrc=wsrc, d=d, nt=nt: e.dma_start(
                out=cb[:, 0:nt * 128].rearrange("p (c j) -> p c j", j=128),
                in_=wsrc[:, d * 128:(d + 1) * 128].rearrange("(c p) j -> p c j", p=128)),
            deps=[ev_quant[j - 1] if j else None])

        def p0qfn(e, w=w):
            qw = w // 4
            for hh in range(2):
                r = _quant_emit2(e, [
                    (cb[:, (2 * hh) * qw:(2 * hh + 1) * qw],
                     qb[:, (2 * hh) * qw:(2 * hh + 1) * qw], p0_T[0], qw),
                    (cb[:, (2 * hh + 1) * qw:(2 * hh + 2) * qw],
                     qb[:, (2 * hh + 1) * qw:(2 * hh + 2) * qw], p0_T[1], qw)])
            return r
        ev_quant[j] = sch.add(
            "vector", p0qfn,
            deps=[ev_load[j], ev_tr[j - 1][-1] if j else None])
        ng = (nt + 3) // 4
        for g in range(ng):
            lo, hi = g * 4, min(nt, g * 4 + 4)

            def trfn(e, lo=lo, hi=hi):
                tb = trp[:].bitcast(BF16)
                for sub in range(lo, hi):
                    r = e.transpose(tb[:, (sub - lo) * 128:(sub - lo + 1) * 128],
                                    qb[:, sub * 128:(sub + 1) * 128], id_sb[:])
                return r
            ev_tr[j].append(sch.add(
                "tensor", trfn,
                deps=[ev_quant[j], ev_id,
                      ev_ev[j][g - 1] if g else (ev_ev[j - 1][-1] if j else None)]))
            ev_ev[j].append(sch.add(
                "vector",
                lambda e, lo=lo, hi=hi: e.tensor_copy(
                    wrow[:, lo * 128:hi * 128],
                    trp[:].bitcast(BF16)[:, 0:(hi - lo) * 128]),
                deps=[ev_tr[j][g], ev_store[j - 1] if j else None]))
        ev_store[j] = sch.dma(
            "gpsimd",
            lambda e, wdst=wdst, d=d, nt=nt: e.dma_start(
                out=wdst[d, :, 0:nt * 128], in_=wrow[:, 0:nt * 128]),
            deps=[ev_ev[j][-1]])
    return record_job, len(jobs)


def emit_p1(nc, sch, id_sb, ev_id, xsrc, wq, wk, wv, kT_scr, v_scr, qT_scr,
            xnqT_scr, ntiles, xnq_scr=None, xnf_scr=None, p0args=None):
    del xnq_scr, xnf_scr
    do_kv = kT_scr is not None
    do_q = qT_scr is not None
    name = "p1" if do_kv else "p1b"

    with ExitStack() as st, nc.Block("ph_" + name) as blk:
        record_p0 = None
        if p0args is not None:
            record_p0, n_p0_jobs = make_p0_recorder(nc, sch, st, id_sb, ev_id, *p0args)

        def sb(nm, shape, dt=F32):
            return st.enter_context(nc.sbuf_tensor(name + "_" + nm, shape, dt))
        wk_sb = sb("wk", [128, 8, D], BF16) if do_kv else None
        wv_sb = sb("wv", [128, 8, D], BF16) if do_kv else None
        wq_sb = sb("wq", [128, 8, D], BF16) if do_q else None
        x_sb = sb("x", [128, 2, 4, D], F32)
        xn_sb = sb("xn", [128, 4, D], BF16)
        xnf_sb = sb("xnf", [128, 4, D], F32) if do_q else None
        xnq_sb = sb("xnq", [128, 4, D], BF16) if do_q else None
        XnT_sb = sb("XnT", [128, 8, 512], BF16)
        kst_sb = sb("kst", [128, 8, 512], BF16) if do_kv else None
        vst_sb = sb("vst", [128, 4, D], BF16) if do_kv else None
        qst_sb = sb("qst", [128, 8, 512], BF16) if do_q else None
        xqT_sb = sb("xqT", [128, 4, D], BF16) if do_q else None
        msum_sb = sb("msum", [128, 4], F32)
        sqsum_sb = sb("sqsum", [128, 4], F32)
        mean_sb = sb("mean", [128, 4], F32)
        var_sb = sb("var", [128, 4], F32)
        rs_sb = sb("rs", [128, 4], F32)
        dump_sb = sb("dump", [128, D], F32)
        sq_sb = sb("sq", [128, 4, D], F32)
        if do_q:
            q_T = [dict(tu=sb(f"qtu{k}", [128, D]), t1=sb(f"qt1{k}", [128, D]),
                        tg=sb(f"qtg{k}", [128, D]), tg2=sb(f"qtg2{k}", [128, D]),
                        bm=sb(f"qbm{k}", [128, D // 16]), sc=sb(f"qsc{k}", [128, D // 16]))
                   for k in range(2)]
        trp = st.enter_context(nc.psum_tensor(name + "_trp", [128, 1024], F32))
        mmp = st.enter_context(nc.psum_tensor(name + "_mmp", [128, 2048], F32))

        wloads = []
        for wsrc, wsb in ((wk, wk_sb), (wv, wv_sb), (wq, wq_sb)):
            if wsb is None:
                continue
            wloads.append(sch.dma(
                "gpsimd",
                lambda e, wsrc=wsrc, wsb=wsb: e.dma_start(
                    out=wsb[...],
                    in_=wsrc[:, :].rearrange("(d p) j -> p d j", p=128))))
        ev_w = wloads[-1]

        ev_x = [None] * ntiles
        ev_stat = [None] * ntiles
        ev_ln = [None] * ntiles
        ev_xn = [None] * ntiles
        ev_trg = []
        ev_evg = []
        ev_kmm = [[None] * HP for _ in range(ntiles)]
        ev_kev = [[None] * HP for _ in range(ntiles)]
        ev_vmm = [[None] * 8 for _ in range(ntiles)]
        ev_vev = [[None] * 8 for _ in range(ntiles)]
        ev_qmm = [[None] * HP for _ in range(ntiles)]
        ev_qev = [[None] * HP for _ in range(ntiles)]
        ev_quant = [None] * ntiles
        ev_kst = [None] * ntiles
        ev_vst = [None] * ntiles
        ev_qst = [None] * ntiles
        ev_xqst = [None] * ntiles

        proj_seq = []

        for t in range(ntiles):
            ev_x[t] = sch.dma(
                "sync",
                lambda e, t=t: e.dma_start(
                    out=x_sb[:, t % 2],
                    in_=xsrc[t * 512:(t + 1) * 512, :].rearrange("(s p) j -> p s j", p=128)),
                deps=[ev_xn[t - 2] if t >= 2 else None])

            def sqfn(e, t=t):
                xt = x_sb[:, t % 2]
                for s in range(4):
                    r = e.activation(sq_sb[:, s], xt[:, s], AF.Square)
                return r
            ev_stat[t] = sch.add("scalar", sqfn,
                                 deps=[ev_x[t], ev_xn[t - 1] if t else None])

            def mvfn(e, t=t):
                xt = x_sb[:, t % 2]
                for s in range(4):
                    e.reduce_sum(msum_sb[:, s:s + 1], xt[:, s], AX.X)
                    e.reduce_sum(sqsum_sb[:, s:s + 1], sq_sb[:, s], AX.X)
                e.drain()
                e.tensor_scalar_mul(mean_sb[:], msum_sb[:], 1.0 / D)
                e.tensor_scalar_mul(var_sb[:], sqsum_sb[:], 1.0 / D)
                e.drain()
                e.tensor_mul(dump_sb[:, 0:4], mean_sb[:], mean_sb[:])
                e.drain()
                e.tensor_sub(var_sb[:], var_sb[:], dump_sb[:, 0:4])
                e.drain()
                return e.tensor_scalar_add(var_sb[:], var_sb[:], EPS)
            ev_mv = sch.add("vector", mvfn, deps=[ev_stat[t]])

            def rsfn(e):
                e.activation(dump_sb[:, 4:8], var_sb[:], AF.Ln)
                e.drain()
                return e.activation(rs_sb[:], dump_sb[:, 4:8], AF.Exp, scale=-0.5)
            ev_ln[t] = sch.add("scalar", rsfn, deps=[ev_mv])

            def xnfn(e, t=t):
                xt = x_sb[:, t % 2]
                for s in range(4):
                    r = e.tensor_scalar((xnf_sb if do_q else xn_sb)[:, s], xt[:, s],
                                        mean_sb[:, s:s + 1], rs_sb[:, s:s + 1],
                                        ALU.subtract, ALU.mult)
                if do_q:
                    for s in range(4):
                        r = e.tensor_copy(xn_sb[:, s], xnf_sb[:, s])
                return r
            ev_xn[t] = sch.add("vector", xnfn, deps=[ev_ln[t]])

            tr_of_tile = []
            for s in range(4):
                gidx = len(ev_trg)

                def trfn(e, t=t, s=s, gidx=gidx):
                    tb = trp[:].bitcast(BF16)
                    for dd in range(8):
                        r = e.transpose(tb[:, (gidx % 2) * 1024 + dd * 128:
                                           (gidx % 2) * 1024 + (dd + 1) * 128],
                                        xn_sb[:, s, dd * 128:(dd + 1) * 128], id_sb[:])
                    return r
                ev_trg.append(sch.add(
                    "tensor", trfn,
                    deps=[ev_xn[t], ev_id,
                          ev_evg[gidx - 2] if gidx >= 2 else None]))

                def evfn(e, s=s, gidx=gidx):
                    return e.tensor_copy(
                        XnT_sb[:, :, s * 128:(s + 1) * 128],
                        trp[:].bitcast(BF16)[:, (gidx % 2) * 1024:(gidx % 2) * 1024 + 1024]
                        .rearrange("p (d j) -> p d j", j=128))
                ev_evg.append(sch.add("vector", evfn, deps=[ev_trg[gidx]]))
                tr_of_tile.append(ev_evg[gidx])
            ev_xnT_done = tr_of_tile[-1]

            def proj(kind, idx, lhs_fn, rhs_fn, evac_dst_fn, store_dep):
                qn = len(proj_seq) % 4

                def mmfn(e, qn=qn, lhs_fn=lhs_fn, rhs_fn=rhs_fn):
                    for dd in range(8):
                        r = e.matmul(mmp[:, qn * 512:(qn + 1) * 512],
                                     lhs_fn(dd), rhs_fn(dd),
                                     start=(dd == 0), stop=(dd == 7))
                    return r
                bank_dep = proj_seq[-4] if len(proj_seq) >= 4 else None
                evm = sch.add("tensor", mmfn,
                              deps=[ev_xnT_done, ev_w, bank_dep])
                eve = sch.add("vector",
                              lambda e, qn=qn, f=evac_dst_fn: e.tensor_copy(
                                  f(), mmp[:, qn * 512:(qn + 1) * 512]),
                              deps=[evm] + store_dep)
                proj_seq.append(eve)
                return evm, eve

            if do_kv:
                for hp in range(HP):
                    ev_kmm[t][hp], ev_kev[t][hp] = proj(
                        "k", hp,
                        lambda dd, hp=hp: wk_sb[:, dd, hp * 128:(hp + 1) * 128],
                        lambda dd: XnT_sb[:, dd],
                        lambda hp=hp: kst_sb[:, hp],
                        [ev_kst[t - 1] if t else None])
                ev_kst[t] = sch.dma(
                    "sync",
                    lambda e, t=t: _multi_dma(
                        e, [(kT_scr[hp, :, t * 512:(t + 1) * 512], kst_sb[:, hp])
                            for hp in range(HP)], sch.sems["dS"]),
                    deps=[ev_kev[t][-1]], n=HP)
                for sv in range(8):
                    s, c2 = divmod(sv, 2)
                    ev_vmm[t][sv], ev_vev[t][sv] = proj(
                        "v", sv,
                        lambda dd, s=s: XnT_sb[:, dd, s * 128:(s + 1) * 128],
                        lambda dd, c2=c2: wv_sb[:, dd, c2 * 512:(c2 + 1) * 512],
                        lambda s=s, c2=c2: vst_sb[:, s, c2 * 512:(c2 + 1) * 512],
                        [ev_vst[t - 1] if t else None])
                ev_vst[t] = sch.dma(
                    "sync",
                    lambda e, t=t: _multi_dma(
                        e, [(v_scr[h, t * 512:(t + 1) * 512, :]
                             .rearrange("(s p) j -> p s j", p=128),
                             vst_sb[:, :, h * 64:(h + 1) * 64]) for h in range(H)],
                        sch.sems["dS"]),
                    deps=[ev_vev[t][-1]], n=H)

            if do_q:
                for hp in range(HP):
                    ev_qmm[t][hp], ev_qev[t][hp] = proj(
                        "q", hp,
                        lambda dd, hp=hp: wq_sb[:, dd, hp * 128:(hp + 1) * 128],
                        lambda dd: XnT_sb[:, dd],
                        lambda hp=hp: qst_sb[:, hp],
                        [ev_qst[t - 1] if t else None])
                ev_qst[t] = sch.dma(
                    "sync",
                    lambda e, t=t: e.dma_start(
                        out=qT_scr[:, :, t * 512:(t + 1) * 512].rearrange("hp p j -> p hp j"),
                        in_=qst_sb[...]),
                    deps=[ev_qev[t][-1]])
                def qfn(e, t=t):
                    for s2 in range(2):
                        r = _quant_emit2(e, [
                            (xnf_sb[:, 2 * s2], xnq_sb[:, 2 * s2], q_T[0], D),
                            (xnf_sb[:, 2 * s2 + 1], xnq_sb[:, 2 * s2 + 1], q_T[1], D)])
                    return r
                ev_quant[t] = sch.add(
                    "vector", qfn,
                    deps=[ev_xqst[t - 1] if t else None])
                ev_qx = []
                for s in range(4):
                    gidx = len(ev_trg)

                    def trfn(e, s=s, gidx=gidx):
                        tb = trp[:].bitcast(BF16)
                        for dd in range(8):
                            r = e.transpose(tb[:, (gidx % 2) * 1024 + dd * 128:
                                               (gidx % 2) * 1024 + (dd + 1) * 128],
                                            xnq_sb[:, s, dd * 128:(dd + 1) * 128], id_sb[:])
                        return r
                    ev_trg.append(sch.add(
                        "tensor", trfn,
                        deps=[ev_quant[t], ev_evg[gidx - 2] if gidx >= 2 else None]))

                    def evfn(e, s=s, gidx=gidx):
                        return e.tensor_copy(
                            xqT_sb[:, s],
                            trp[:].bitcast(BF16)[:, (gidx % 2) * 1024:(gidx % 2) * 1024 + 1024])
                    ev_evg.append(sch.add("vector", evfn, deps=[ev_trg[gidx]]))
                    ev_qx.append(ev_evg[gidx])
                ev_xqst[t] = sch.dma(
                    "sync",
                    lambda e, t=t: _multi_dma(
                        e, [(xnqT_scr[t * 4 + s], xqT_sb[:, s]) for s in range(4)],
                        sch.sems["dS"]),
                    deps=[ev_qx[-1]], n=4)
            if record_p0 is not None:
                record_p0(t)
        if record_p0 is not None:
            for j in range(ntiles, n_p0_jobs):
                record_p0(j)
        sch.emit(blk)


def _multi_dma(e, pairs, sem):
    for dst, src in pairs[:-1]:
        e.dma_start(out=dst, in_=src).then_inc(sem, 16)
    dst, src = pairs[-1]
    return e.dma_start(out=dst, in_=src)


def emit_p2(nc, sch, kT_scr, v_scr, qT_scr, oT_scr):
    with ExitStack() as st, nc.Block("ph_p2") as blk:
        def sb(nm, shape, dt=BF16):
            return st.enter_context(nc.sbuf_tensor("p2_" + nm, shape, dt))
        kT_sb = sb("kT", [128, 2, S])
        vp_sb = sb("vp", [128, 2, 2, 2 * NPAIR * 65])
        qt_sb = sb("qt", [128, 2, 512])
        p_sb = sb("p", [128, 2, 2, 1024])
        oraw_sb = sb("oraw", [128, Q], F32)
        l_sb = sb("l", [64, Q], F32)
        rlb_sb = sb("rlb", [128, Q], F32)
        oT_out = sb("oTo", [128, Q])
        s_ps = st.enter_context(nc.psum_tensor("p2_sps", [128, 3072], F32))
        o_ps = st.enter_context(nc.psum_tensor("p2_ops", [128, 1024], F32))

        ev_scr = ("dS", sch.ctrs["dS"].v)

        ev_ones = sch.add("vector", lambda e: e.memset(vp_sb[...], 1.0))

        ev_loadkv = [None] * (HP + 1)
        ev_mul = [None] * (HP + 1)
        ev_store = [None] * (HP + 1)
        ev_pvend = [None] * (HP + 1)
        prev_exp_last = [None, None]
        prev_tail = [None, None]

        for hp in range(HP):
            par = hp % 2

            NSPL = max(1, (2 * NPAIR) // 32)
            CSP = 2 * NPAIR // NSPL

            def kvfn(e, hp=hp, par=par):
                e.dma_start(out=kT_sb[:, par, :], in_=kT_scr[hp]).then_inc(sch.sems["dG"], 16)
                for h in range(2):
                    for s4 in range(NSPL):
                        r = e.dma_start(
                            out=vp_sb[:, par, h].rearrange("p (c s) -> p c s", s=65)
                            [:, s4 * CSP:(s4 + 1) * CSP, 0:64],
                            in_=v_scr[2 * hp + h, s4 * CSP * 128:(s4 + 1) * CSP * 128, :]
                            .rearrange("(c p) j -> p c j", p=128))
                        if not (h == 1 and s4 == NSPL - 1):
                            r.then_inc(sch.sems["dG"], 16)
                return r
            ev_loadkv[hp] = sch.dma(
                "gpsimd", kvfn, n=1 + 2 * NSPL,
                deps=[ev_scr, ev_ones, ev_pvend[hp - 2] if hp >= 2 else None])


            NTRIP0 = NPAIR
            ev_q = [None] * NQT
            GTOT = NQT * 2 * NTRIP0
            sflat = [None] * GTOT
            expflat = [None] * GTOT
            sbufcol = [0] * GTOT
            ev_pv = [[[None] * NTRIP0 for _ in range(NQT)] for _ in range(2)]
            ev_exp = [[[None] * NTRIP0 for _ in range(NQT)] for _ in range(2)]
            ev_evac = [[None] * NQT for _ in range(2)]

            for qt in range(NQT):
                ev_q[qt] = sch.dma(
                    "sync",
                    lambda e, hp=hp, qt=qt, par=qt % 2: e.dma_start(
                        out=qt_sb[:, par, :],
                        in_=qT_scr[hp][:, qt * 512:(qt + 1) * 512]),
                    deps=[ev_scr,
                          sflat[(qt - 1) * 2 * NTRIP0 - 1] if qt >= 2 else
                          (ev_pvend[hp - 1] if hp >= 1 else None)])

                NTRIP = NPAIR
                for i in range(NTRIP + 1):
                    if i < NTRIP:
                        gq = qt * 2 * NTRIP + 2 * i
                        bA, bB = gq % 3, (gq + 1) % 3
                        sbufcol[gq], sbufcol[gq + 1] = bA, bB

                        def sfn(e, par=par, qt=qt, i=i, bA=bA, bB=bB):
                            for ci in range(2):
                                c = 2 * i + ci
                                for h, buf in ((0, bA), (1, bB)):
                                    r = e.matmul(
                                        s_ps[:, buf * 1024 + ci * 512:
                                             buf * 1024 + ci * 512 + 512],
                                        kT_sb[h * 64:(h + 1) * 64, par,
                                              c * 128:(c + 1) * 128],
                                        qt_sb[h * 64:(h + 1) * 64, qt % 2, :],
                                        start=True, stop=True,
                                        tile_position=(h * 64, 0))
                            return r
                        deps = [ev_loadkv[hp], ev_q[qt]]
                        for back in (3, 2):
                            if gq - back >= 0:
                                deps.append(expflat[gq - back])
                            elif hp >= 1:
                                deps.append(prev_exp_last[1])
                        ev = sch.add("tensor", sfn, deps=deps)
                        sflat[gq] = sflat[gq + 1] = ev
                    for h in range(2):
                        if i >= 1:
                            def pvfn(e, par=par, i=i, h=h):
                                for ci in range(2):
                                    c = 2 * (i - 1) + ci
                                    r = e.matmul(
                                        o_ps[0:65, h * 512:(h + 1) * 512],
                                        vp_sb[:, par, h, c * 65:(c + 1) * 65],
                                        p_sb[:, h, (i - 1) % 2,
                                             ci * 512:(ci + 1) * 512],
                                        start=(c == 0), stop=(c == 2 * NPAIR - 1))
                                return r
                            deps = [ev_exp[h][qt][i - 1]]
                            if i == 1:
                                if qt >= 1:
                                    deps.append(ev_evac[h][qt - 1])
                                elif hp >= 1:
                                    deps.append(ev_mul[hp - 1])
                            ev_pv[h][qt][i - 1] = sch.add("tensor", pvfn, deps=deps)

                    for h in range(2):
                        if i < NTRIP:
                            gq = qt * 2 * NTRIP + 2 * i + h
                            buf = sbufcol[gq]

                            def xfn(e, h=h, i=i, buf=buf):
                                return e.activation(p_sb[:, h, i % 2, :],
                                                    s_ps[:, buf * 1024:
                                                         buf * 1024 + 1024],
                                                    AF.Exp, scale=0.125)
                            deps = [sflat[gq]]
                            if i >= 2:
                                deps.append(ev_pv[h][qt][i - 2])
                            elif qt >= 1:
                                deps.append(ev_pv[h][qt - 1][NTRIP - 1 - (1 - i)])
                            else:
                                deps.append(prev_tail[h])
                            ev_exp[h][qt][i] = sch.add("scalar", xfn, deps=deps)
                            expflat[gq] = ev_exp[h][qt][i]

                for h in range(2):
                    def evfn(e, h=h, qt=qt):
                        e.tensor_copy(l_sb[h * 32:h * 32 + 1, qt * 512:(qt + 1) * 512],
                                      o_ps[64:65, h * 512:(h + 1) * 512])
                        return e.tensor_copy(
                            oraw_sb[h * 64:(h + 1) * 64, qt * 512:(qt + 1) * 512],
                            o_ps[0:64, h * 512:(h + 1) * 512])
                    ev_evac[h][qt] = sch.add(
                        "vector", evfn,
                        deps=[ev_pv[h][qt][NTRIP0 - 1],
                              ev_mul[hp - 1] if (qt == 0 and hp >= 1) else None])

            ev_pvend[hp] = ev_pv[1][NQT - 1][NTRIP0 - 1]
            prev_exp_last = [ev_exp[0][NQT - 1][NTRIP0 - 1], ev_exp[1][NQT - 1][NTRIP0 - 1]]
            prev_tail = [ev_pv[0][NQT - 1][NTRIP0 - 1], ev_pv[1][NQT - 1][NTRIP0 - 1]]

            def recipfn(e):
                e.drain()
                e.reciprocal(l_sb[0:1, :], l_sb[0:1, :])
                return e.reciprocal(l_sb[32:33, :], l_sb[32:33, :])
            ev_recip = sch.add("vector", recipfn,
                               deps=[ev_evac[0][NQT - 1], ev_evac[1][NQT - 1]])

            def rlbfn(e):
                e.dma_start(out=rlb_sb[0:64, :],
                            in_=l_sb[0:1, :].unsqueeze(1).broadcast_to([1, 64, Q])
                            ).then_inc(sch.sems["dS"], 16)
                return e.dma_start(out=rlb_sb[64:128, :],
                                   in_=l_sb[32:33, :].unsqueeze(1).broadcast_to([1, 64, Q]))
            ev_rlb = sch.dma("sync", rlbfn, n=2, deps=[ev_recip])

            ev_mul[hp] = sch.add(
                "vector",
                lambda e: e.tensor_mul(oT_out[:, :], oraw_sb[:, :], rlb_sb[:, :]),
                deps=[ev_rlb, ev_store[hp - 1] if hp >= 1 else None])
            ev_store[hp] = sch.dma(
                "sync",
                lambda e, hp=hp: e.dma_start(out=oT_scr[hp], in_=oT_out[:, :]),
                deps=[ev_mul[hp]])
        sch.emit(blk)


def emit_p3f(nc, sch, id_sb, ev_id, wgT_scr, wuT_scr, xnqT_scr, hqT_scr):
    IC_OFF = [0, 512, 1024, 1536, 2048, 2560]
    IC_W = [512, 512, 512, 512, 512, 256]
    with ExitStack() as st, nc.Block("ph_p3f") as blk:
        def sb(nm, shape, dt=BF16):
            return st.enter_context(nc.sbuf_tensor("p3f_" + nm, shape, dt))
        wgT_sb = sb("wg", [128, 8, I_FFN])
        wuT_sb = sb("wu", [128, 8, I_FFN])
        xnqTt_sb = sb("xnqTt", [128, 2, D])
        t_sb = sb("t", [128, 2, 512], F32)
        t2_sb = sb("t2", [128, 512], F32)
        uf_sb = sb("uf", [128, 2, 512], F32)
        a_sb = sb("a", [128, 512], F32)
        h_sb = sb("h", [128, 512], F32)
        hq_sb = sb("hq", [128, 2, 512])
        hqTs_sb = sb("hqTs", [128, 2, NIC, 128])
        p3_T = [dict(tu=sb(f"qtu{k}", [128, 256], F32), t1=sb(f"qt1{k}", [128, 256], F32),
                     tg=sb(f"qtg{k}", [128, 256], F32), tg2=sb(f"qtg2{k}", [128, 256], F32),
                     bm=sb(f"qbm{k}", [128, 16], F32), sc=sb(f"qsc{k}", [128, 16], F32))
                for k in range(2)]
        gu_ps = st.enter_context(nc.psum_tensor("p3f_gu", [128, 3072], F32))
        tr_ps = st.enter_context(nc.psum_tensor("p3f_tr", [128, 1024], F32))

        ev_scr = ("dS", sch.ctrs["dS"].v)

        def wloadfn(e):
            e.dma_start(out=wgT_sb[...],
                        in_=wgT_scr[...].rearrange("d p j -> p d j")
                        ).then_inc(sch.sems["dS"], 16)
            return e.dma_start(out=wuT_sb[...],
                               in_=wuT_scr[...].rearrange("d p j -> p d j"))
        ev_w = sch.dma("sync", wloadfn, n=2, deps=[ev_scr])

        ev_x = [None] * NTC
        ev_g = [[None] * 6 for _ in range(NTC)]
        ev_u = [[None] * 6 for _ in range(NTC)]
        ev_t = [[None] * 6 for _ in range(NTC)]
        ev_h = [[None] * 6 for _ in range(NTC)]
        ev_htr = [[None] * 6 for _ in range(NTC)]
        ev_hev = [[None] * 6 for _ in range(NTC)]
        ev_hst = [None] * NTC

        for tc in range(NTC):
            ev_x[tc] = sch.dma(
                "sync",
                lambda e, tc=tc: e.dma_start(out=xnqTt_sb[:, tc % 2, :],
                                             in_=xnqT_scr[tc]),
                deps=[ev_scr,
                      ev_g[tc - 2][5] if tc >= 2 else None,
                      ev_u[tc - 2][5] if tc >= 2 else None])

            for ic in range(6):
                off, w = IC_OFF[ic], IC_W[ic]
                gq = (ic % 3) * 1024
                uq = (ic % 3) * 1024 + 512

                def gfn(e, tc=tc, off=off, w=w, gq=gq):
                    for dd in range(8):
                        r = e.matmul(gu_ps[:, gq:gq + w],
                                     xnqTt_sb[:, tc % 2, dd * 128:(dd + 1) * 128],
                                     wgT_sb[:, dd, off:off + w],
                                     start=(dd == 0), stop=(dd == 7))
                    return r
                ev_g[tc][ic] = sch.add(
                    "tensor", gfn,
                    deps=[ev_x[tc], ev_w,
                          ev_h[tc][ic - 3] if ic >= 3 else
                          (ev_h[tc - 1][ic + 3] if tc >= 1 else None)])

                def ufn(e, tc=tc, off=off, w=w, uq=uq):
                    for dd in range(8):
                        r = e.matmul(gu_ps[:, uq:uq + w],
                                     xnqTt_sb[:, tc % 2, dd * 128:(dd + 1) * 128],
                                     wuT_sb[:, dd, off:off + w],
                                     start=(dd == 0), stop=(dd == 7))
                    return r
                ev_u[tc][ic] = sch.add(
                    "tensor", ufn,
                    deps=[ev_h[tc][ic - 3] if ic >= 3 else
                          (ev_h[tc - 1][ic + 3] if tc >= 1 else None)])

                def tfn(e, w=w, gq=gq, uq=uq, ic=ic):
                    e.copy(uf_sb[:, ic % 2, 0:w], gu_ps[:, uq:uq + w])
                    return e.activation(t_sb[:, ic % 2, 0:w], gu_ps[:, gq:gq + w],
                                        AF.Tanh, scale=0.5)
                ev_t[tc][ic] = sch.add(
                    "scalar", tfn,
                    deps=[ev_g[tc][ic], ev_u[tc][ic],
                          ev_h[tc][ic - 2] if ic >= 2 else
                          (ev_h[tc - 1][ic + 4] if tc >= 1 else None)])

                def hfn(e, w=w, gq=gq, ic=ic):
                    e.tensor_scalar(t2_sb[:, 0:w], t_sb[:, ic % 2, 0:w], 0.5, 0.5,
                                    ALU.mult, ALU.add)
                    e.tensor_mul(a_sb[:, 0:w], gu_ps[:, gq:gq + w], uf_sb[:, ic % 2, 0:w])
                    e.drain()
                    e.tensor_mul(h_sb[:, 0:w], t2_sb[:, 0:w], a_sb[:, 0:w])
                    e.drain()
                    h1 = w // 2
                    return _quant_emit2(e, [
                        (h_sb[:, 0:h1], hq_sb[:, ic % 2, 0:h1], p3_T[0], h1),
                        (h_sb[:, h1:w], hq_sb[:, ic % 2, h1:w], p3_T[1], w - h1)])
                ev_h[tc][ic] = sch.add(
                    "vector", hfn,
                    deps=[ev_t[tc][ic],
                          ev_htr[tc][ic - 2] if ic >= 2 else
                          (ev_htr[tc - 1][ic + 4] if tc >= 1 else None)])

                def htrfn(e, w=w, ic=ic):
                    tb = tr_ps[:].bitcast(BF16)
                    for subq in range(w // 128):
                        r = e.transpose(tb[:, (ic % 2) * 1024 + subq * 128:
                                           (ic % 2) * 1024 + (subq + 1) * 128],
                                        hq_sb[:, ic % 2, subq * 128:(subq + 1) * 128],
                                        id_sb[:])
                    return r
                ev_htr[tc][ic] = sch.add(
                    "tensor", htrfn,
                    deps=[ev_h[tc][ic], ev_id,
                          ev_hev[tc][ic - 2] if ic >= 2 else
                          (ev_hev[tc - 1][ic + 4] if tc >= 1 else None)])
                ev_hev[tc][ic] = sch.add(
                    "vector",
                    lambda e, tc=tc, ic=ic, w=w: e.tensor_copy(
                        hqTs_sb[:, tc % 2, ic * 4:ic * 4 + w // 128, :],
                        tr_ps[:].bitcast(BF16)[:, (ic % 2) * 1024:(ic % 2) * 1024 + w]
                        .rearrange("p (c j) -> p c j", j=128)),
                    deps=[ev_htr[tc][ic],
                          ev_hst[tc - 2] if tc >= 2 else None])

            ev_hst[tc] = sch.dma(
                "sync",
                lambda e, tc=tc: e.dma_start(
                    out=hqT_scr[tc],
                    in_=hqTs_sb[:, tc % 2].rearrange("p c j -> p (c j)")),
                deps=[ev_hev[tc][5]])
        sch.emit(blk)


def emit_p3t(nc, sch, xq, wo, wdT_scr, hqT_scr, oT_scr, out):
    with ExitStack() as st, nc.Block("ph_p3t") as blk:
        def sb(nm, shape, dt=BF16):
            return st.enter_context(nc.sbuf_tensor("p3t_" + nm, shape, dt))
        wdT_sb = sb("wd", [128, NIC, D])
        wo_sb = sb("wo", [128, 8, D])
        hqTt_sb = sb("hqTt", [128, 2, NIC, 128])
        oTt_sb = sb("oTt", [128, 2, 8, 128])
        xq_sb = sb("xq", [128, 2, D], F32)
        out_sb = sb("outv", [128, 2, D], F32)
        out_ps = st.enter_context(nc.psum_tensor("p3t_out", [128, 2048], F32))

        ev_scr = ("dS", sch.ctrs["dS"].v)
        ev_w = sch.dma(
            "sync",
            lambda e: e.dma_start(out=wdT_sb[...],
                                  in_=wdT_scr[...].rearrange("d p j -> p d j")),
            deps=[ev_scr])
        ev_wo = sch.dma(
            "gpsimd",
            lambda e: e.dma_start(out=wo_sb[...],
                                  in_=wo[:, :].rearrange("(d p) j -> p d j", p=128)))

        ev_x = [None] * NTC
        ev_ox = [None] * NTC
        ev_do = [None] * NTC
        ev_out = [None] * NTC
        ev_ost = [None] * NTC

        for tc in range(NTC):
            def xfn(e, tc=tc):
                e.dma_start(out=hqTt_sb[:, tc % 2],
                            in_=hqT_scr[tc].rearrange("p (c j) -> p c j", j=128)
                            ).then_inc(sch.sems["dS"], 16)
                return e.dma_start(out=xq_sb[:, tc % 2, :],
                                   in_=xq[tc * 128:(tc + 1) * 128, :])
            ev_x[tc] = sch.dma("sync", xfn, n=2,
                               deps=[ev_scr,
                                     ev_do[tc - 2] if tc >= 2 else None,
                                     ev_out[tc - 2] if tc >= 2 else None])
            ev_ox[tc] = sch.dma(
                "gpsimd",
                lambda e, tc=tc: e.dma_start(
                    out=oTt_sb[:, tc % 2],
                    in_=oT_scr[:, :, tc * 128:(tc + 1) * 128]
                    .rearrange("hp p j -> p hp j")),
                deps=[ev_scr, ev_do[tc - 2] if tc >= 2 else None])

            def dofn(e, tc=tc):
                po = (tc % 2) * 1024
                for d2 in range(2):
                    for i in range(NIC):
                        e.matmul(out_ps[:, po + d2 * 512:po + d2 * 512 + 512],
                                 hqTt_sb[:, tc % 2, i, :],
                                 wdT_sb[:, i, d2 * 512:(d2 + 1) * 512],
                                 start=(i == 0), stop=False)
                    for hp in range(HP):
                        r = e.matmul(out_ps[:, po + d2 * 512:po + d2 * 512 + 512],
                                     oTt_sb[:, tc % 2, hp, :],
                                     wo_sb[:, hp, d2 * 512:(d2 + 1) * 512],
                                     start=False, stop=(hp == HP - 1))
                return r
            ev_do[tc] = sch.add(
                "tensor", dofn,
                deps=[ev_x[tc], ev_ox[tc], ev_w, ev_wo,
                      ev_out[tc - 2] if tc >= 2 else None])

            ev_out[tc] = sch.add(
                "vector",
                lambda e, tc=tc: e.tensor_add(
                    out_sb[:, tc % 2], out_ps[:, (tc % 2) * 1024:(tc % 2) * 1024 + 1024],
                    xq_sb[:, tc % 2]),
                deps=[ev_do[tc], ev_ost[tc - 2] if tc >= 2 else None])
            ev_ost[tc] = sch.dma(
                "sync",
                lambda e, tc=tc: e.dma_start(out=out[tc * 128:(tc + 1) * 128, :],
                                             in_=out_sb[:, tc % 2]),
                deps=[ev_out[tc]])
        sch.emit(blk)


_CACHED = {}


def kernel(L0_attn_x, L0_attn_w_q, L0_attn_w_k, L0_attn_w_v, L0_attn_w_o,
           L0_ffn_w_gate, L0_ffn_w_up, L0_ffn_w_down):
    import ml_dtypes
    x = np.ascontiguousarray(np.asarray(L0_attn_x, dtype=np.float32))
    ws = dict(
        wq=np.ascontiguousarray(np.asarray(L0_attn_w_q, np.float32)),
        wk=np.ascontiguousarray(np.asarray(L0_attn_w_k, np.float32)),
        wv=np.ascontiguousarray(np.asarray(L0_attn_w_v, np.float32)),
        wo=np.ascontiguousarray(np.asarray(L0_attn_w_o, np.float32)),
        wg=np.ascontiguousarray(np.asarray(L0_ffn_w_gate, np.float32)),
        wu=np.ascontiguousarray(np.asarray(L0_ffn_w_up, np.float32)),
        wd=np.ascontiguousarray(np.asarray(L0_ffn_w_down, np.float32)),
    )
    ident = np.eye(128, dtype=ml_dtypes.bfloat16)
    if "nc" not in _CACHED:
        _CACHED["nc"] = build_program()
    nc = _CACHED["nc"]
    in_maps = []
    for c in range(8):
        b, qs = c // 4, (c % 4) * Q
        in_maps.append(dict(xb=x[b], xq=x[b][qs:qs + Q], identb=ident, **ws))
    res = run_bass_kernel_spmd(nc, in_maps, core_ids=list(range(8)))
    out = np.empty((2, S, D), dtype=np.float32)
    for c in range(8):
        b, qs = c // 4, (c % 4) * Q
        out[b, qs:qs + Q] = res.results[c]["out"]
    return out


# version 18
# speedup vs baseline: 1.0952x; 1.0952x over previous
import numpy as np
from contextlib import ExitStack

import concourse.bass as bass
import concourse.mybir as mybir
from concourse.bass_utils import run_bass_kernel_spmd

F32 = mybir.dt.float32
BF16 = mybir.dt.bfloat16
I32 = mybir.dt.int32
AF = mybir.ActivationFunctionType
ALU = mybir.AluOpType
AX = mybir.AxisListType

S = 16384
Q = 4096
D = 1024
H = 16
DH = 64
HP = 8
I_FFN = 2816
NIC = 22
NT1 = S // 512
NT1B = Q // 512
NTC = Q // 128
NQT = Q // 512
NPAIR = S // 256
EPS = 1e-5


def set_dims(s, q):
    global S, Q, NT1, NT1B, NTC, NQT, NPAIR
    S, Q = s, q
    NT1 = S // 512
    NT1B = Q // 512
    NTC = Q // 128
    NQT = Q // 512
    NPAIR = S // 256

DBG = dict(p1=False, p1x=False, p2=False, phases=("p0", "p1", "p1b", "p2", "p3"))

ENGINES = ("tensor", "vector", "scalar", "sync", "gpsimd")
DMA_SEM_OF = {"sync": "dS", "gpsimd": "dG", "scalar": "dA"}
ENG_SEM_OF = {"tensor": "cT", "vector": "cV", "scalar": "cA", "gpsimd": "cG", "sync": "cS"}


class Sched:

    def __init__(self, nc, sems, ctrs):
        self.nc = nc
        self.sems = sems
        self.ctrs = ctrs
        self.ops = {e: [] for e in ENGINES}

    def add(self, eng, fn, deps=(), dma=False, n=1):
        semname = DMA_SEM_OF[eng] if dma else ENG_SEM_OF[eng]
        ctr = self.ctrs[semname]
        ctr.v += 16 * n if dma else 1
        ev = (semname, ctr.v)
        deps = [d for d in deps if d is not None]
        self.ops[eng].append((fn, deps, semname, ctr.v, dma))
        return ev

    def dma(self, eng, fn, deps=(), n=1):
        return self.add(eng, fn, deps, dma=True, n=n)

    def emit(self, blk):
        for eng in ENGINES:
            ops = self.ops[eng]
            if not ops:
                continue

            def body(e, ops=ops, eng=eng):
                waited = {}
                own = ENG_SEM_OF[eng]
                for fn, deps, semname, tgt, dma in ops:
                    for (dsem, dtgt) in deps:
                        if dsem == own:
                            continue
                        if waited.get(dsem, -1) >= dtgt:
                            continue
                        e.wait_ge(self.sems[dsem], dtgt)
                        waited[dsem] = dtgt
                    last = fn(e)
                    assert last is not None, f"op on {eng} returned None"
                    last.then_inc(self.sems[semname], 16 if dma else 1)
            getattr(blk, eng)(body)
        self.ops = {e: [] for e in ENGINES}


class _Ctr:
    __slots__ = ("v",)
    def __init__(self):
        self.v = 0


def _quant_emit(vector, x_f32, q_bf16, tu, t1, tg, tg2, bm, sc, w):
    nblk = w // 16
    x3 = x_f32.rearrange("p (n b) -> p n b", b=16)
    D_ = vector.drain
    vector.tensor_reduce(bm.unsqueeze(2), x3, AX.X, ALU.max, apply_absolute_value=True)
    vector.tensor_single_scalar(tu.bitcast(I32), x_f32.bitcast(I32), 0x7FFFFFFF, ALU.bitwise_and)
    D_()
    vector.tensor_scalar(sc, bm, float(np.float32(1.0) / np.float32(6.0)), 1e-12, ALU.mult, ALU.max)
    D_()
    vector.reciprocal(bm, sc)
    D_()
    rb = bm.unsqueeze(2).broadcast_to([128, nblk, 16])
    u3 = tu.rearrange("p (n b) -> p n b", b=16)
    vector.tensor_tensor(u3, u3, rb, ALU.mult)
    D_()
    vector.tensor_scalar_mul(t1, tu, float(1 << 22))
    D_()
    vector.tensor_add(tg, t1, tu)
    D_()
    vector.tensor_sub(tg, tg, t1)
    D_()
    vector.tensor_scalar_max(tg, tg, 1.0)
    vector.tensor_single_scalar(t1, tu, 0.75, ALU.is_le)
    D_()
    vector.tensor_scalar(tu, tu, 0.25, 0.5, ALU.is_gt, ALU.mult)
    D_()
    vector.select(tg2, t1.bitcast(I32), tu, tg)
    D_()
    sb3 = sc.unsqueeze(2).broadcast_to([128, nblk, 16])
    g3 = tg2.rearrange("p (n b) -> p n b", b=16)
    vector.tensor_tensor(g3, g3, sb3, ALU.mult)
    vector.tensor_single_scalar(t1.bitcast(I32), x_f32.bitcast(I32), -0x80000000, ALU.bitwise_and)
    D_()
    vector.tensor_tensor(tu.bitcast(I32), tg2.bitcast(I32), t1.bitcast(I32), ALU.bitwise_or)
    D_()
    return vector.tensor_copy(q_bf16, tu)




def _quant_emit2(vector, jobs):
    if len(jobs) == 1:
        x, q, T, w = jobs[0]
        return _quant_emit(vector, x, q, T["tu"][:, 0:w], T["t1"][:, 0:w],
                           T["tg"][:, 0:w], T["tg2"][:, 0:w],
                           T["bm"][:, 0:w // 16], T["sc"][:, 0:w // 16], w)
    steps = []
    for x_f32, q_bf16, T, w in jobs:
        nblk = w // 16
        tu, t1, tg, tg2 = (T[k][:, 0:w] for k in ("tu", "t1", "tg", "tg2"))
        bm, sc = T["bm"][:, 0:nblk], T["sc"][:, 0:nblk]
        x3 = x_f32.rearrange("p (n b) -> p n b", b=16)
        rb = bm.unsqueeze(2).broadcast_to([128, nblk, 16])
        u3 = tu.rearrange("p (n b) -> p n b", b=16)
        sb3 = sc.unsqueeze(2).broadcast_to([128, nblk, 16])
        g3 = tg2.rearrange("p (n b) -> p n b", b=16)
        c16 = float(np.float32(1.0) / np.float32(6.0))
        steps.append([
            lambda v, a=bm, b=x3: v.tensor_reduce(a.unsqueeze(2), b, AX.X, ALU.max,
                                                  apply_absolute_value=True),
            lambda v, a=tu, b=x_f32: v.tensor_single_scalar(
                a.bitcast(I32), b.bitcast(I32), 0x7FFFFFFF, ALU.bitwise_and),
            lambda v, a=sc, b=bm: v.tensor_scalar(a, b, c16, 1e-12, ALU.mult, ALU.max),
            lambda v, a=bm, b=sc: v.reciprocal(a, b),
            lambda v, a=u3, b=rb: v.tensor_tensor(a, a, b, ALU.mult),
            lambda v, a=t1, b=tu: v.tensor_scalar_mul(a, b, float(1 << 22)),
            lambda v, a=tg, b=t1, c=tu: v.tensor_add(a, b, c),
            lambda v, a=tg, b=t1: v.tensor_sub(a, a, b),
            lambda v, a=tg: v.tensor_scalar_max(a, a, 1.0),
            lambda v, a=t1, b=tu: v.tensor_single_scalar(a, b, 0.75, ALU.is_le),
            lambda v, a=tu: v.tensor_scalar(a, a, 0.25, 0.5, ALU.is_gt, ALU.mult),
            lambda v, a=tg2, b=t1, c=tu, d=tg: v.select(a, b.bitcast(I32), c, d),
            lambda v, a=g3, b=sb3: v.tensor_tensor(a, a, b, ALU.mult),
            lambda v, a=t1, b=x_f32: v.tensor_single_scalar(
                a.bitcast(I32), b.bitcast(I32), -0x80000000, ALU.bitwise_and),
            lambda v, a=tu, b=tg2, c=t1: v.tensor_tensor(
                a.bitcast(I32), b.bitcast(I32), c.bitcast(I32), ALU.bitwise_or),
            lambda v, a=q_bf16, b=tu: v.tensor_copy(a, b),
        ])
    r = None
    for k in range(len(steps[0])):
        for s in steps:
            r = s[k](vector)
    return r


def build_program():
    nc = bass.Bass("TRN2", target_bir_lowering=False)

    xb = nc.dram_tensor("xb", [S, D], F32, kind="ExternalInput")
    xq = nc.dram_tensor("xq", [Q, D], F32, kind="ExternalInput")
    wq = nc.dram_tensor("wq", [D, D], F32, kind="ExternalInput")
    wk = nc.dram_tensor("wk", [D, D], F32, kind="ExternalInput")
    wv = nc.dram_tensor("wv", [D, D], F32, kind="ExternalInput")
    wo = nc.dram_tensor("wo", [D, D], F32, kind="ExternalInput")
    wg = nc.dram_tensor("wg", [I_FFN, D], F32, kind="ExternalInput")
    wu = nc.dram_tensor("wu", [I_FFN, D], F32, kind="ExternalInput")
    wd = nc.dram_tensor("wd", [D, I_FFN], F32, kind="ExternalInput")
    identb = nc.dram_tensor("identb", [128, 128], BF16, kind="ExternalInput")
    out = nc.dram_tensor("out", [Q, D], F32, kind="ExternalOutput")

    kT_scr = nc.dram_tensor("kT_scr", [HP, 128, S], BF16)
    v_scr = nc.dram_tensor("v_scr", [H, S, DH], BF16)
    qT_scr = nc.dram_tensor("qT_scr", [HP, 128, Q], BF16)
    xnqT_scr = nc.dram_tensor("xnqT_scr", [NTC, 128, D], BF16)
    oT_scr = nc.dram_tensor("oT_scr", [NTC, 128, HP, 128], BF16)
    wgT_scr = nc.dram_tensor("wgT_scr", [8, 128, I_FFN], BF16)
    wuT_scr = nc.dram_tensor("wuT_scr", [8, 128, I_FFN], BF16)
    wdT_scr = nc.dram_tensor("wdT_scr", [NIC, 128, D], BF16)
    hqT_scr = nc.dram_tensor("hqT_scr", [NTC, 128, NIC * 128], BF16)

    xnq_scr = None
    xnf_scr = None
    dbg = {}
    if DBG["p1"]:
        for nm, ref in (("d_kT", kT_scr), ("d_v", v_scr), ("d_qT", qT_scr),
                        ("d_xnqT", xnqT_scr), ("d_wgT", wgT_scr), ("d_wuT", wuT_scr),
                        ("d_wdT", wdT_scr)):
            dbg[nm] = (nc.dram_tensor(nm, list(ref.shape), BF16, kind="ExternalOutput"), ref)
    if DBG["p2"]:
        dbg["d_oT"] = (nc.dram_tensor("d_oT", [HP, 128, Q], BF16, kind="ExternalOutput"), oT_scr)
        nc._p2_oraw = nc.dram_tensor("d_oraw", [128, Q], F32, kind="ExternalOutput")
        nc._p2_l = nc.dram_tensor("d_l", [64, Q], F32, kind="ExternalOutput")
        nc._p2_p = nc.dram_tensor("d_p", [128, 2, 2, 1024], BF16, kind="ExternalOutput")
        nc._p2_rlb = nc.dram_tensor("d_rlb", [128, Q], F32, kind="ExternalOutput")
        nc._p2_vp = nc.dram_tensor("d_vp", [128, 2, 2 * (S // 256) * 65], BF16, kind="ExternalOutput")
        nc._p2_ev = nc.dram_tensor("d_ev", [64, 512], F32, kind="ExternalOutput")

    with ExitStack() as gst:
        sems = {nm: gst.enter_context(nc.semaphore(nm))
                for nm in ("dS", "dG", "dA", "dV", "cT", "cA", "cV", "cG", "cS")}
        ctrs = {nm: _Ctr() for nm in sems}
        sch = Sched(nc, sems, ctrs)
        id_sb = gst.enter_context(nc.sbuf_tensor("id_sb", [128, 128], BF16))

        with nc.Block("ph_boot") as blk:
            ev_id = sch.dma("sync", lambda e: e.dma_start(out=id_sb[:], in_=identb[:, :]))
            sch.emit(blk)

        ev_p1 = None
        if "p1" in DBG["phases"]:
            p0args = (wg, wu, wd, wgT_scr, wuT_scr, wdT_scr) if "p0" in DBG["phases"] else None
            emit_p1(nc, sch, id_sb, ev_id, xb, wq, wk, wv,
                    kT_scr, v_scr, None, None, NT1, p0args=p0args)
        if "p1b" in DBG["phases"]:
            emit_p1(nc, sch, id_sb, ev_id, xq, wq, wk, wv,
                    None, None, qT_scr, xnqT_scr, NT1B, xnq_scr=xnq_scr, xnf_scr=xnf_scr)
        if "p3" in DBG["phases"]:
            emit_p3f(nc, sch, id_sb, ev_id, wgT_scr, wuT_scr, xnqT_scr, hqT_scr)
        if "p2" in DBG["phases"]:
            emit_p2(nc, sch, kT_scr, v_scr, qT_scr, oT_scr)
        if "p3" in DBG["phases"]:
            emit_p3t(nc, sch, xq, wo, wdT_scr, hqT_scr, oT_scr, out)

        if dbg:
            with nc.Block("ph_dbg") as blk:
                for nm, (dtensor, src) in dbg.items():
                    sch.dma("sync", lambda e, d=dtensor, s=src: e.dma_start(out=d[...], in_=s[...]))
                sch.emit(blk)

        with nc.Block("ph_fin") as blk:
            @blk.sync
            def _(sync):
                sync.wait_ge(sems["dS"], ctrs["dS"].v)
                sync.wait_ge(sems["dG"], ctrs["dG"].v)
    return nc


def make_p0_recorder(nc, sch, st, id_sb, ev_id, wg, wu, wd, wgT_scr, wuT_scr, wdT_scr):
    jobs = [(wg, wgT_scr, d, NIC) for d in range(8)] + \
           [(wu, wuT_scr, d, NIC) for d in range(8)] + \
           [(wd, wdT_scr, i, 8) for i in range(NIC)]
    cb = st.enter_context(nc.sbuf_tensor("p0_cb", [128, NIC * 128], F32))
    qb = st.enter_context(nc.sbuf_tensor("p0_qb", [128, NIC * 128], BF16))
    p0_T = []
    for k in range(2):
        p0_T.append(dict(
            tu=st.enter_context(nc.sbuf_tensor(f"p0_tu{k}", [128, 704], F32)),
            t1=st.enter_context(nc.sbuf_tensor(f"p0_t1{k}", [128, 704], F32)),
            tg=st.enter_context(nc.sbuf_tensor(f"p0_tg{k}", [128, 704], F32)),
            tg2=st.enter_context(nc.sbuf_tensor(f"p0_tg2{k}", [128, 704], F32)),
            bm=st.enter_context(nc.sbuf_tensor(f"p0_bm{k}", [128, 44], F32)),
            sc=st.enter_context(nc.sbuf_tensor(f"p0_sc{k}", [128, 44], F32))))
    wrow = st.enter_context(nc.sbuf_tensor("p0_wrow", [128, I_FFN], BF16))
    trp = st.enter_context(nc.psum_tensor("p0_trps", [128, 512], F32))

    state = dict(ev_load=[None] * len(jobs), ev_quant=[None] * len(jobs),
                 ev_tr=[[] for _ in jobs], ev_ev=[[] for _ in jobs],
                 ev_store=[None] * len(jobs))

    def record_job(j):
        if j >= len(jobs):
            return
        wsrc, wdst, d, nt = jobs[j]
        w = nt * 128
        ev_load, ev_quant = state["ev_load"], state["ev_quant"]
        ev_tr, ev_ev, ev_store = state["ev_tr"], state["ev_ev"], state["ev_store"]
        ev_load[j] = sch.dma(
            "gpsimd",
            lambda e, wsrc=wsrc, d=d, nt=nt: e.dma_start(
                out=cb[:, 0:nt * 128].rearrange("p (c j) -> p c j", j=128),
                in_=wsrc[:, d * 128:(d + 1) * 128].rearrange("(c p) j -> p c j", p=128)),
            deps=[ev_quant[j - 1] if j else None])

        def p0qfn(e, w=w):
            qw = w // 4
            for hh in range(2):
                r = _quant_emit2(e, [
                    (cb[:, (2 * hh) * qw:(2 * hh + 1) * qw],
                     qb[:, (2 * hh) * qw:(2 * hh + 1) * qw], p0_T[0], qw),
                    (cb[:, (2 * hh + 1) * qw:(2 * hh + 2) * qw],
                     qb[:, (2 * hh + 1) * qw:(2 * hh + 2) * qw], p0_T[1], qw)])
            return r
        ev_quant[j] = sch.add(
            "vector", p0qfn,
            deps=[ev_load[j], ev_tr[j - 1][-1] if j else None])
        ng = (nt + 3) // 4
        for g in range(ng):
            lo, hi = g * 4, min(nt, g * 4 + 4)

            def trfn(e, lo=lo, hi=hi):
                tb = trp[:].bitcast(BF16)
                for sub in range(lo, hi):
                    r = e.transpose(tb[:, (sub - lo) * 128:(sub - lo + 1) * 128],
                                    qb[:, sub * 128:(sub + 1) * 128], id_sb[:])
                return r
            ev_tr[j].append(sch.add(
                "tensor", trfn,
                deps=[ev_quant[j], ev_id,
                      ev_ev[j][g - 1] if g else (ev_ev[j - 1][-1] if j else None)]))
            ev_ev[j].append(sch.add(
                "vector",
                lambda e, lo=lo, hi=hi: e.tensor_copy(
                    wrow[:, lo * 128:hi * 128],
                    trp[:].bitcast(BF16)[:, 0:(hi - lo) * 128]),
                deps=[ev_tr[j][g], ev_store[j - 1] if j else None]))
        ev_store[j] = sch.dma(
            "gpsimd",
            lambda e, wdst=wdst, d=d, nt=nt: e.dma_start(
                out=wdst[d, :, 0:nt * 128], in_=wrow[:, 0:nt * 128]),
            deps=[ev_ev[j][-1]])
    return record_job, len(jobs)


def emit_p1(nc, sch, id_sb, ev_id, xsrc, wq, wk, wv, kT_scr, v_scr, qT_scr,
            xnqT_scr, ntiles, xnq_scr=None, xnf_scr=None, p0args=None):
    del xnq_scr, xnf_scr
    do_kv = kT_scr is not None
    do_q = qT_scr is not None
    name = "p1" if do_kv else "p1b"

    with ExitStack() as st, nc.Block("ph_" + name) as blk:
        record_p0 = None
        if p0args is not None:
            record_p0, n_p0_jobs = make_p0_recorder(nc, sch, st, id_sb, ev_id, *p0args)

        def sb(nm, shape, dt=F32):
            return st.enter_context(nc.sbuf_tensor(name + "_" + nm, shape, dt))
        wk_sb = sb("wk", [128, 8, D], BF16) if do_kv else None
        wv_sb = sb("wv", [128, 8, D], BF16) if do_kv else None
        wq_sb = sb("wq", [128, 8, D], BF16) if do_q else None
        x_sb = sb("x", [128, 2, 4, D], F32)
        xn_sb = sb("xn", [128, 4, D], BF16)
        xnf_sb = sb("xnf", [128, 4, D], F32) if do_q else None
        xnq_sb = sb("xnq", [128, 4, D], BF16) if do_q else None
        XnT_sb = sb("XnT", [128, 8, 512], BF16)
        kst_sb = sb("kst", [128, 8, 512], BF16) if do_kv else None
        vst_sb = sb("vst", [128, 4, D], BF16) if do_kv else None
        qst_sb = sb("qst", [128, 8, 512], BF16) if do_q else None
        xqT_sb = sb("xqT", [128, 4, D], BF16) if do_q else None
        msum_sb = sb("msum", [128, 4], F32)
        sqsum_sb = sb("sqsum", [128, 4], F32)
        mean_sb = sb("mean", [128, 4], F32)
        var_sb = sb("var", [128, 4], F32)
        rs_sb = sb("rs", [128, 4], F32)
        dump_sb = sb("dump", [128, D], F32)
        sq_sb = sb("sq", [128, 4, D], F32)
        if do_q:
            q_T = [dict(tu=sb(f"qtu{k}", [128, D]), t1=sb(f"qt1{k}", [128, D]),
                        tg=sb(f"qtg{k}", [128, D]), tg2=sb(f"qtg2{k}", [128, D]),
                        bm=sb(f"qbm{k}", [128, D // 16]), sc=sb(f"qsc{k}", [128, D // 16]))
                   for k in range(2)]
        trp = st.enter_context(nc.psum_tensor(name + "_trp", [128, 1024], F32))
        mmp = st.enter_context(nc.psum_tensor(name + "_mmp", [128, 2048], F32))

        wloads = []
        for wsrc, wsb in ((wk, wk_sb), (wv, wv_sb), (wq, wq_sb)):
            if wsb is None:
                continue
            wloads.append(sch.dma(
                "gpsimd",
                lambda e, wsrc=wsrc, wsb=wsb: e.dma_start(
                    out=wsb[...],
                    in_=wsrc[:, :].rearrange("(d p) j -> p d j", p=128))))
        ev_w = wloads[-1]

        ev_x = [None] * ntiles
        ev_stat = [None] * ntiles
        ev_ln = [None] * ntiles
        ev_xn = [None] * ntiles
        ev_trg = []
        ev_evg = []
        ev_kmm = [[None] * HP for _ in range(ntiles)]
        ev_kev = [[None] * HP for _ in range(ntiles)]
        ev_vmm = [[None] * 8 for _ in range(ntiles)]
        ev_vev = [[None] * 8 for _ in range(ntiles)]
        ev_qmm = [[None] * HP for _ in range(ntiles)]
        ev_qev = [[None] * HP for _ in range(ntiles)]
        ev_quant = [None] * ntiles
        ev_kst = [None] * ntiles
        ev_vst = [None] * ntiles
        ev_qst = [None] * ntiles
        ev_xqst = [None] * ntiles

        proj_seq = []

        for t in range(ntiles):
            ev_x[t] = sch.dma(
                "sync",
                lambda e, t=t: e.dma_start(
                    out=x_sb[:, t % 2],
                    in_=xsrc[t * 512:(t + 1) * 512, :].rearrange("(s p) j -> p s j", p=128)),
                deps=[ev_xn[t - 2] if t >= 2 else None])

            def sqfn(e, t=t):
                xt = x_sb[:, t % 2]
                for s in range(4):
                    r = e.activation(sq_sb[:, s], xt[:, s], AF.Square)
                return r
            ev_stat[t] = sch.add("scalar", sqfn,
                                 deps=[ev_x[t], ev_xn[t - 1] if t else None])

            def mvfn(e, t=t):
                xt = x_sb[:, t % 2]
                for s in range(4):
                    e.reduce_sum(msum_sb[:, s:s + 1], xt[:, s], AX.X)
                    e.reduce_sum(sqsum_sb[:, s:s + 1], sq_sb[:, s], AX.X)
                e.drain()
                e.tensor_scalar_mul(mean_sb[:], msum_sb[:], 1.0 / D)
                e.tensor_scalar_mul(var_sb[:], sqsum_sb[:], 1.0 / D)
                e.drain()
                e.tensor_mul(dump_sb[:, 0:4], mean_sb[:], mean_sb[:])
                e.drain()
                e.tensor_sub(var_sb[:], var_sb[:], dump_sb[:, 0:4])
                e.drain()
                return e.tensor_scalar_add(var_sb[:], var_sb[:], EPS)
            ev_mv = sch.add("vector", mvfn, deps=[ev_stat[t]])

            def rsfn(e):
                e.activation(dump_sb[:, 4:8], var_sb[:], AF.Ln)
                e.drain()
                return e.activation(rs_sb[:], dump_sb[:, 4:8], AF.Exp, scale=-0.5)
            ev_ln[t] = sch.add("scalar", rsfn, deps=[ev_mv])

            def xnfn(e, t=t):
                xt = x_sb[:, t % 2]
                for s in range(4):
                    r = e.tensor_scalar((xnf_sb if do_q else xn_sb)[:, s], xt[:, s],
                                        mean_sb[:, s:s + 1], rs_sb[:, s:s + 1],
                                        ALU.subtract, ALU.mult)
                if do_q:
                    for s in range(4):
                        r = e.tensor_copy(xn_sb[:, s], xnf_sb[:, s])
                return r
            ev_xn[t] = sch.add("vector", xnfn, deps=[ev_ln[t]])

            tr_of_tile = []
            for s in range(4):
                gidx = len(ev_trg)

                def trfn(e, t=t, s=s, gidx=gidx):
                    tb = trp[:].bitcast(BF16)
                    for dd in range(8):
                        r = e.transpose(tb[:, (gidx % 2) * 1024 + dd * 128:
                                           (gidx % 2) * 1024 + (dd + 1) * 128],
                                        xn_sb[:, s, dd * 128:(dd + 1) * 128], id_sb[:])
                    return r
                ev_trg.append(sch.add(
                    "tensor", trfn,
                    deps=[ev_xn[t], ev_id,
                          ev_evg[gidx - 2] if gidx >= 2 else None]))

                def evfn(e, s=s, gidx=gidx):
                    return e.tensor_copy(
                        XnT_sb[:, :, s * 128:(s + 1) * 128],
                        trp[:].bitcast(BF16)[:, (gidx % 2) * 1024:(gidx % 2) * 1024 + 1024]
                        .rearrange("p (d j) -> p d j", j=128))
                ev_evg.append(sch.add("vector", evfn, deps=[ev_trg[gidx]]))
                tr_of_tile.append(ev_evg[gidx])
            ev_xnT_done = tr_of_tile[-1]

            def proj(kind, idx, lhs_fn, rhs_fn, evac_dst_fn, store_dep):
                qn = len(proj_seq) % 4

                def mmfn(e, qn=qn, lhs_fn=lhs_fn, rhs_fn=rhs_fn):
                    for dd in range(8):
                        r = e.matmul(mmp[:, qn * 512:(qn + 1) * 512],
                                     lhs_fn(dd), rhs_fn(dd),
                                     start=(dd == 0), stop=(dd == 7))
                    return r
                bank_dep = proj_seq[-4] if len(proj_seq) >= 4 else None
                evm = sch.add("tensor", mmfn,
                              deps=[ev_xnT_done, ev_w, bank_dep])
                eve = sch.add("vector",
                              lambda e, qn=qn, f=evac_dst_fn: e.tensor_copy(
                                  f(), mmp[:, qn * 512:(qn + 1) * 512]),
                              deps=[evm] + store_dep)
                proj_seq.append(eve)
                return evm, eve

            if do_kv:
                for hp in range(HP):
                    ev_kmm[t][hp], ev_kev[t][hp] = proj(
                        "k", hp,
                        lambda dd, hp=hp: wk_sb[:, dd, hp * 128:(hp + 1) * 128],
                        lambda dd: XnT_sb[:, dd],
                        lambda hp=hp: kst_sb[:, hp],
                        [ev_kst[t - 1] if t else None])
                ev_kst[t] = sch.dma(
                    "sync",
                    lambda e, t=t: _multi_dma(
                        e, [(kT_scr[hp, :, t * 512:(t + 1) * 512], kst_sb[:, hp])
                            for hp in range(HP)], sch.sems["dS"]),
                    deps=[ev_kev[t][-1]], n=HP)
                for sv in range(8):
                    s, c2 = divmod(sv, 2)
                    ev_vmm[t][sv], ev_vev[t][sv] = proj(
                        "v", sv,
                        lambda dd, s=s: XnT_sb[:, dd, s * 128:(s + 1) * 128],
                        lambda dd, c2=c2: wv_sb[:, dd, c2 * 512:(c2 + 1) * 512],
                        lambda s=s, c2=c2: vst_sb[:, s, c2 * 512:(c2 + 1) * 512],
                        [ev_vst[t - 1] if t else None])
                ev_vst[t] = sch.dma(
                    "sync",
                    lambda e, t=t: _multi_dma(
                        e, [(v_scr[h, t * 512:(t + 1) * 512, :]
                             .rearrange("(s p) j -> p s j", p=128),
                             vst_sb[:, :, h * 64:(h + 1) * 64]) for h in range(H)],
                        sch.sems["dS"]),
                    deps=[ev_vev[t][-1]], n=H)

            if do_q:
                for hp in range(HP):
                    ev_qmm[t][hp], ev_qev[t][hp] = proj(
                        "q", hp,
                        lambda dd, hp=hp: wq_sb[:, dd, hp * 128:(hp + 1) * 128],
                        lambda dd: XnT_sb[:, dd],
                        lambda hp=hp: qst_sb[:, hp],
                        [ev_qst[t - 1] if t else None])
                ev_qst[t] = sch.dma(
                    "sync",
                    lambda e, t=t: e.dma_start(
                        out=qT_scr[:, :, t * 512:(t + 1) * 512].rearrange("hp p j -> p hp j"),
                        in_=qst_sb[...]),
                    deps=[ev_qev[t][-1]])
                def qfn(e, t=t):
                    for s2 in range(2):
                        r = _quant_emit2(e, [
                            (xnf_sb[:, 2 * s2], xnq_sb[:, 2 * s2], q_T[0], D),
                            (xnf_sb[:, 2 * s2 + 1], xnq_sb[:, 2 * s2 + 1], q_T[1], D)])
                    return r
                ev_quant[t] = sch.add(
                    "vector", qfn,
                    deps=[ev_xqst[t - 1] if t else None])
                ev_qx = []
                for s in range(4):
                    gidx = len(ev_trg)

                    def trfn(e, s=s, gidx=gidx):
                        tb = trp[:].bitcast(BF16)
                        for dd in range(8):
                            r = e.transpose(tb[:, (gidx % 2) * 1024 + dd * 128:
                                               (gidx % 2) * 1024 + (dd + 1) * 128],
                                            xnq_sb[:, s, dd * 128:(dd + 1) * 128], id_sb[:])
                        return r
                    ev_trg.append(sch.add(
                        "tensor", trfn,
                        deps=[ev_quant[t], ev_evg[gidx - 2] if gidx >= 2 else None]))

                    def evfn(e, s=s, gidx=gidx):
                        return e.tensor_copy(
                            xqT_sb[:, s],
                            trp[:].bitcast(BF16)[:, (gidx % 2) * 1024:(gidx % 2) * 1024 + 1024])
                    ev_evg.append(sch.add("vector", evfn, deps=[ev_trg[gidx]]))
                    ev_qx.append(ev_evg[gidx])
                ev_xqst[t] = sch.dma(
                    "sync",
                    lambda e, t=t: _multi_dma(
                        e, [(xnqT_scr[t * 4 + s], xqT_sb[:, s]) for s in range(4)],
                        sch.sems["dS"]),
                    deps=[ev_qx[-1]], n=4)
            if record_p0 is not None:
                record_p0(t)
        if record_p0 is not None:
            for j in range(ntiles, n_p0_jobs):
                record_p0(j)
        sch.emit(blk)


def _multi_dma(e, pairs, sem):
    for dst, src in pairs[:-1]:
        e.dma_start(out=dst, in_=src).then_inc(sem, 16)
    dst, src = pairs[-1]
    return e.dma_start(out=dst, in_=src)


def emit_p2(nc, sch, kT_scr, v_scr, qT_scr, oT_scr):
    with ExitStack() as st, nc.Block("ph_p2") as blk:
        def sb(nm, shape, dt=BF16):
            return st.enter_context(nc.sbuf_tensor("p2_" + nm, shape, dt))
        kT_sb = sb("kT", [128, 2, S])
        vp_sb = sb("vp", [128, 2, 2, 2 * NPAIR * 65])
        qt_sb = sb("qt", [128, 2, 512])
        p_sb = sb("p", [128, 2, 2, 1024])
        oraw_sb = sb("oraw", [128, Q], F32)
        l_sb = sb("l", [64, Q], F32)
        rlb_sb = sb("rlb", [128, Q], F32)
        oT_out = sb("oTo", [128, Q])
        s_ps = st.enter_context(nc.psum_tensor("p2_sps", [128, 3072], F32))
        o_ps = st.enter_context(nc.psum_tensor("p2_ops", [128, 1024], F32))

        ev_scr = ("dS", sch.ctrs["dS"].v)

        ev_ones = sch.add("vector", lambda e: e.memset(vp_sb[...], 1.0))

        ev_loadkv = [None] * (HP + 1)
        ev_mul = [None] * (HP + 1)
        ev_store = [None] * (HP + 1)
        ev_pvend = [None] * (HP + 1)
        prev_exp_last = [None, None]
        prev_tail = [None, None]
        prev_sflat = None
        pending_tail = [None]

        for hp in range(HP):
            par = hp % 2

            NSPL = max(1, (2 * NPAIR) // 32)
            CSP = 2 * NPAIR // NSPL

            def kvfn(e, hp=hp, par=par):
                e.dma_start(out=kT_sb[:, par, :], in_=kT_scr[hp]).then_inc(sch.sems["dG"], 16)
                for h in range(2):
                    for s4 in range(NSPL):
                        r = e.dma_start(
                            out=vp_sb[:, par, h].rearrange("p (c s) -> p c s", s=65)
                            [:, s4 * CSP:(s4 + 1) * CSP, 0:64],
                            in_=v_scr[2 * hp + h, s4 * CSP * 128:(s4 + 1) * CSP * 128, :]
                            .rearrange("(c p) j -> p c j", p=128))
                        if not (h == 1 and s4 == NSPL - 1):
                            r.then_inc(sch.sems["dG"], 16)
                return r
            ev_loadkv[hp] = sch.dma(
                "gpsimd", kvfn, n=1 + 2 * NSPL,
                deps=[ev_scr, ev_ones, ev_pvend[hp - 2] if hp >= 2 else None])


            NTRIP0 = NPAIR
            ev_q = [None] * NQT
            GTOT = NQT * 2 * NTRIP0
            sflat = [None] * GTOT
            expflat = [None] * GTOT
            sbufcol = [0] * GTOT
            ev_pv = [[[None] * NTRIP0 for _ in range(NQT)] for _ in range(2)]
            ev_exp = [[[None] * NTRIP0 for _ in range(NQT)] for _ in range(2)]
            ev_evac = [[None] * NQT for _ in range(2)]

            for qt in range(NQT):
                ev_q[qt] = sch.dma(
                    "sync",
                    lambda e, hp=hp, qt=qt, par=qt % 2: e.dma_start(
                        out=qt_sb[:, par, :],
                        in_=qT_scr[hp][:, qt * 512:(qt + 1) * 512]),
                    deps=[ev_scr,
                          sflat[(qt - 1) * 2 * NTRIP0 - 1] if qt >= 2 else
                          (prev_sflat[(NQT - 1 + qt) * 2 * NPAIR - 1] if hp >= 1
                           else None)])
                if qt == 0 and pending_tail[0] is not None:
                    pending_tail[0]()
                    pending_tail[0] = None

                NTRIP = NPAIR
                for i in range(NTRIP + 1):
                    if i < NTRIP:
                        gq = qt * 2 * NTRIP + 2 * i
                        bA, bB = gq % 3, (gq + 1) % 3
                        sbufcol[gq], sbufcol[gq + 1] = bA, bB

                        def sfn(e, par=par, qt=qt, i=i, bA=bA, bB=bB):
                            for ci in range(2):
                                c = 2 * i + ci
                                for h, buf in ((0, bA), (1, bB)):
                                    r = e.matmul(
                                        s_ps[:, buf * 1024 + ci * 512:
                                             buf * 1024 + ci * 512 + 512],
                                        kT_sb[h * 64:(h + 1) * 64, par,
                                              c * 128:(c + 1) * 128],
                                        qt_sb[h * 64:(h + 1) * 64, qt % 2, :],
                                        start=True, stop=True,
                                        tile_position=(h * 64, 0))
                            return r
                        deps = [ev_loadkv[hp], ev_q[qt]]
                        for back in (3, 2):
                            if gq - back >= 0:
                                deps.append(expflat[gq - back])
                            elif hp >= 1:
                                deps.append(prev_exp_last[1])
                        ev = sch.add("tensor", sfn, deps=deps)
                        sflat[gq] = sflat[gq + 1] = ev
                    for h in range(2):
                        if i >= 1:
                            def pvfn(e, par=par, i=i, h=h):
                                for ci in range(2):
                                    c = 2 * (i - 1) + ci
                                    r = e.matmul(
                                        o_ps[0:65, h * 512:(h + 1) * 512],
                                        vp_sb[:, par, h, c * 65:(c + 1) * 65],
                                        p_sb[:, h, (i - 1) % 2,
                                             ci * 512:(ci + 1) * 512],
                                        start=(c == 0), stop=(c == 2 * NPAIR - 1))
                                return r
                            deps = [ev_exp[h][qt][i - 1]]
                            if i == 1:
                                if qt >= 1:
                                    deps.append(ev_evac[h][qt - 1])
                                elif hp >= 1:
                                    deps.append(ev_mul[hp - 1])
                            ev_pv[h][qt][i - 1] = sch.add("tensor", pvfn, deps=deps)

                    for h in range(2):
                        if i < NTRIP:
                            gq = qt * 2 * NTRIP + 2 * i + h
                            buf = sbufcol[gq]

                            def xfn(e, h=h, i=i, buf=buf):
                                return e.activation(p_sb[:, h, i % 2, :],
                                                    s_ps[:, buf * 1024:
                                                         buf * 1024 + 1024],
                                                    AF.Exp, scale=0.125)
                            deps = [sflat[gq]]
                            if i >= 2:
                                deps.append(ev_pv[h][qt][i - 2])
                            elif qt >= 1:
                                deps.append(ev_pv[h][qt - 1][NTRIP - 1 - (1 - i)])
                            else:
                                deps.append(prev_tail[h])
                            ev_exp[h][qt][i] = sch.add("scalar", xfn, deps=deps)
                            expflat[gq] = ev_exp[h][qt][i]

                for h in range(2):
                    def evfn(e, h=h, qt=qt):
                        e.tensor_copy(l_sb[h * 32:h * 32 + 1, qt * 512:(qt + 1) * 512],
                                      o_ps[64:65, h * 512:(h + 1) * 512])
                        return e.tensor_copy(
                            oraw_sb[h * 64:(h + 1) * 64, qt * 512:(qt + 1) * 512],
                            o_ps[0:64, h * 512:(h + 1) * 512])
                    ev_evac[h][qt] = sch.add(
                        "vector", evfn,
                        deps=[ev_pv[h][qt][NTRIP0 - 1],
                              ev_mul[hp - 1] if (qt == 0 and hp >= 1) else None])

            ev_pvend[hp] = ev_pv[1][NQT - 1][NTRIP0 - 1]
            prev_sflat = sflat
            prev_exp_last = [ev_exp[0][NQT - 1][NTRIP0 - 1], ev_exp[1][NQT - 1][NTRIP0 - 1]]
            prev_tail = [ev_pv[0][NQT - 1][NTRIP0 - 1], ev_pv[1][NQT - 1][NTRIP0 - 1]]

            def make_tail(hp=hp, ev_ev0=ev_evac[0][NQT - 1], ev_ev1=ev_evac[1][NQT - 1]):
                def rec():
                    def recipfn(e):
                        e.drain()
                        e.reciprocal(l_sb[0:1, :], l_sb[0:1, :])
                        return e.reciprocal(l_sb[32:33, :], l_sb[32:33, :])
                    ev_recip = sch.add("vector", recipfn, deps=[ev_ev0, ev_ev1])

                    def rlbfn(e):
                        e.dma_start(out=rlb_sb[0:64, :],
                                    in_=l_sb[0:1, :].unsqueeze(1)
                                    .broadcast_to([1, 64, Q])
                                    ).then_inc(sch.sems["dS"], 16)
                        return e.dma_start(out=rlb_sb[64:128, :],
                                           in_=l_sb[32:33, :].unsqueeze(1)
                                           .broadcast_to([1, 64, Q]))
                    ev_rlb = sch.dma("sync", rlbfn, n=2, deps=[ev_recip])

                    ev_mul[hp] = sch.add(
                        "vector",
                        lambda e: e.tensor_mul(oT_out[:, :], oraw_sb[:, :], rlb_sb[:, :]),
                        deps=[ev_rlb, ev_store[hp - 1] if hp >= 1 else None])
                    ev_store[hp] = sch.dma(
                        "sync",
                        lambda e, hp=hp: e.dma_start(
                            out=oT_scr[:, :, hp, :].rearrange("c p j -> p c j"),
                            in_=oT_out[:, :].rearrange("p (c j) -> p c j", j=128)),
                        deps=[ev_mul[hp]])
                return rec
            pending_tail[0] = make_tail()
        pending_tail[0]()
        pending_tail[0] = None
        sch.emit(blk)


def emit_p3f(nc, sch, id_sb, ev_id, wgT_scr, wuT_scr, xnqT_scr, hqT_scr):
    IC_OFF = [0, 512, 1024, 1536, 2048, 2560]
    IC_W = [512, 512, 512, 512, 512, 256]
    with ExitStack() as st, nc.Block("ph_p3f") as blk:
        def sb(nm, shape, dt=BF16):
            return st.enter_context(nc.sbuf_tensor("p3f_" + nm, shape, dt))
        wgT_sb = sb("wg", [128, 8, I_FFN])
        wuT_sb = sb("wu", [128, 8, I_FFN])
        xnqTt_sb = sb("xnqTt", [128, 2, D])
        t_sb = sb("t", [128, 2, 512], F32)
        t2_sb = sb("t2", [128, 512], F32)
        uf_sb = sb("uf", [128, 2, 512], F32)
        a_sb = sb("a", [128, 512], F32)
        h_sb = sb("h", [128, 512], F32)
        hq_sb = sb("hq", [128, 3, 512])
        hqTs_sb = sb("hqTs", [128, 2, NIC, 128])
        p3_T = [dict(tu=sb(f"qtu{k}", [128, 256], F32), t1=sb(f"qt1{k}", [128, 256], F32),
                     tg=sb(f"qtg{k}", [128, 256], F32), tg2=sb(f"qtg2{k}", [128, 256], F32),
                     bm=sb(f"qbm{k}", [128, 16], F32), sc=sb(f"qsc{k}", [128, 16], F32))
                for k in range(2)]
        gu_ps = st.enter_context(nc.psum_tensor("p3f_gu", [128, 3072], F32))
        tr_ps = st.enter_context(nc.psum_tensor("p3f_tr", [128, 1024], F32))

        ev_scr = ("dS", sch.ctrs["dS"].v)

        def wloadfn(e):
            e.dma_start(out=wgT_sb[...],
                        in_=wgT_scr[...].rearrange("d p j -> p d j")
                        ).then_inc(sch.sems["dS"], 16)
            return e.dma_start(out=wuT_sb[...],
                               in_=wuT_scr[...].rearrange("d p j -> p d j"))
        ev_w = sch.dma("sync", wloadfn, n=2, deps=[ev_scr])

        ev_x = [None] * NTC
        ev_g = [[None] * 6 for _ in range(NTC)]
        ev_u = [[None] * 6 for _ in range(NTC)]
        ev_t = [[None] * 6 for _ in range(NTC)]
        ev_h = [[None] * 6 for _ in range(NTC)]
        ev_htr = [[None] * 6 for _ in range(NTC)]
        ev_hev = [[None] * 6 for _ in range(NTC)]
        ev_hst = [None] * NTC

        def _rec_tr(tc, ic):
            w = IC_W[ic]

            def htrfn(e, w=w, ic=ic, tc=tc):
                tb = tr_ps[:].bitcast(BF16)
                for subq in range(w // 128):
                    r = e.transpose(tb[:, (ic % 2) * 1024 + subq * 128:
                                       (ic % 2) * 1024 + (subq + 1) * 128],
                                    hq_sb[:, ic % 3, subq * 128:(subq + 1) * 128],
                                    id_sb[:])
                return r
            ev_htr[tc][ic] = sch.add(
                "tensor", htrfn,
                deps=[ev_h[tc][ic], ev_id,
                      ev_hev[tc][ic - 2] if ic >= 2 else
                      (ev_hev[tc - 1][ic + 4] if tc >= 1 else None)])
            ev_hev[tc][ic] = sch.add(
                "vector",
                lambda e, tc=tc, ic=ic, w=w: e.tensor_copy(
                    hqTs_sb[:, tc % 2, ic * 4:ic * 4 + w // 128, :],
                    tr_ps[:].bitcast(BF16)[:, (ic % 2) * 1024:(ic % 2) * 1024 + w]
                    .rearrange("p (c j) -> p c j", j=128)),
                deps=[ev_htr[tc][ic],
                      ev_hst[tc - 2] if tc >= 2 else None])
            if ic == 5:
                ev_hst[tc] = sch.dma(
                    "sync",
                    lambda e, tc=tc: e.dma_start(
                        out=hqT_scr[tc],
                        in_=hqTs_sb[:, tc % 2].rearrange("p c j -> p (c j)")),
                    deps=[ev_hev[tc][5]])

        for tc in range(NTC):
            ev_x[tc] = sch.dma(
                "sync",
                lambda e, tc=tc: e.dma_start(out=xnqTt_sb[:, tc % 2, :],
                                             in_=xnqT_scr[tc]),
                deps=[ev_scr,
                      ev_g[tc - 2][5] if tc >= 2 else None,
                      ev_u[tc - 2][5] if tc >= 2 else None])

            for ic in range(6):
                off, w = IC_OFF[ic], IC_W[ic]
                gq = (ic % 3) * 1024
                uq = (ic % 3) * 1024 + 512

                def gfn(e, tc=tc, off=off, w=w, gq=gq):
                    for dd in range(8):
                        r = e.matmul(gu_ps[:, gq:gq + w],
                                     xnqTt_sb[:, tc % 2, dd * 128:(dd + 1) * 128],
                                     wgT_sb[:, dd, off:off + w],
                                     start=(dd == 0), stop=(dd == 7))
                    return r
                ev_g[tc][ic] = sch.add(
                    "tensor", gfn,
                    deps=[ev_x[tc], ev_w,
                          ev_h[tc][ic - 3] if ic >= 3 else
                          (ev_h[tc - 1][ic + 3] if tc >= 1 else None)])

                def ufn(e, tc=tc, off=off, w=w, uq=uq):
                    for dd in range(8):
                        r = e.matmul(gu_ps[:, uq:uq + w],
                                     xnqTt_sb[:, tc % 2, dd * 128:(dd + 1) * 128],
                                     wuT_sb[:, dd, off:off + w],
                                     start=(dd == 0), stop=(dd == 7))
                    return r
                ev_u[tc][ic] = sch.add(
                    "tensor", ufn,
                    deps=[ev_h[tc][ic - 3] if ic >= 3 else
                          (ev_h[tc - 1][ic + 3] if tc >= 1 else None)])

                def tfn(e, w=w, gq=gq, uq=uq, ic=ic):
                    e.copy(uf_sb[:, ic % 2, 0:w], gu_ps[:, uq:uq + w])
                    return e.activation(t_sb[:, ic % 2, 0:w], gu_ps[:, gq:gq + w],
                                        AF.Tanh, scale=0.5)
                ev_t[tc][ic] = sch.add(
                    "scalar", tfn,
                    deps=[ev_g[tc][ic], ev_u[tc][ic],
                          ev_h[tc][ic - 2] if ic >= 2 else
                          (ev_h[tc - 1][ic + 4] if tc >= 1 else None)])

                def hfn(e, w=w, gq=gq, ic=ic):
                    e.tensor_scalar(t2_sb[:, 0:w], t_sb[:, ic % 2, 0:w], 0.5, 0.5,
                                    ALU.mult, ALU.add)
                    e.tensor_mul(a_sb[:, 0:w], gu_ps[:, gq:gq + w], uf_sb[:, ic % 2, 0:w])
                    e.drain()
                    e.tensor_mul(h_sb[:, 0:w], t2_sb[:, 0:w], a_sb[:, 0:w])
                    e.drain()
                    h1 = w // 2
                    return _quant_emit2(e, [
                        (h_sb[:, 0:h1], hq_sb[:, ic % 3, 0:h1], p3_T[0], h1),
                        (h_sb[:, h1:w], hq_sb[:, ic % 3, h1:w], p3_T[1], w - h1)])
                ev_h[tc][ic] = sch.add(
                    "vector", hfn,
                    deps=[ev_t[tc][ic],
                          ev_htr[tc][ic - 3] if ic >= 3 else
                          (ev_htr[tc - 1][ic + 3] if tc >= 1 else None)])

                ltc, lic = (tc, ic - 2) if ic >= 2 else (tc - 1, ic + 4)
                if ltc >= 0:
                    _rec_tr(ltc, lic)
            if tc == NTC - 1:
                _rec_tr(tc, 4)
                _rec_tr(tc, 5)
        sch.emit(blk)


def emit_p3t(nc, sch, xq, wo, wdT_scr, hqT_scr, oT_scr, out):
    with ExitStack() as st, nc.Block("ph_p3t") as blk:
        def sb(nm, shape, dt=BF16):
            return st.enter_context(nc.sbuf_tensor("p3t_" + nm, shape, dt))
        wdT_sb = sb("wd", [128, NIC, D])
        wo_sb = sb("wo", [128, 8, D])
        hqTt_sb = sb("hqTt", [128, 3, NIC, 128])
        oTt_sb = sb("oTt", [128, 3, 8, 128])
        xq_sb = sb("xq", [128, 3, D], F32)
        out_sb = sb("outv", [128, 3, D], F32)
        out_ps = st.enter_context(nc.psum_tensor("p3t_out", [128, 3072], F32))

        ev_scr = ("dS", sch.ctrs["dS"].v)
        ev_w = sch.dma(
            "sync",
            lambda e: e.dma_start(out=wdT_sb[...],
                                  in_=wdT_scr[...].rearrange("d p j -> p d j")),
            deps=[ev_scr])
        ev_wo = sch.dma(
            "gpsimd",
            lambda e: e.dma_start(out=wo_sb[...],
                                  in_=wo[:, :].rearrange("(d p) j -> p d j", p=128)))

        ev_x = [None] * NTC
        ev_ox = [None] * NTC
        ev_do = [None] * NTC
        ev_out = [None] * NTC
        ev_ost = [None] * NTC

        for tc in range(NTC):
            def xfn(e, tc=tc):
                e.dma_start(out=hqTt_sb[:, tc % 3],
                            in_=hqT_scr[tc].rearrange("p (c j) -> p c j", j=128)
                            ).then_inc(sch.sems["dS"], 16)
                return e.dma_start(out=xq_sb[:, tc % 3, :],
                                   in_=xq[tc * 128:(tc + 1) * 128, :])
            ev_x[tc] = sch.dma("sync", xfn, n=2,
                               deps=[ev_scr,
                                     ev_do[tc - 3] if tc >= 3 else None,
                                     ev_out[tc - 3] if tc >= 3 else None])
            ev_ox[tc] = sch.dma(
                "gpsimd",
                lambda e, tc=tc: e.dma_start(
                    out=oTt_sb[:, tc % 3],
                    in_=oT_scr[tc].rearrange("p c j -> p (c j)")),
                deps=[ev_scr, ev_do[tc - 3] if tc >= 3 else None])

            def dofn(e, tc=tc):
                po = (tc % 3) * 1024
                for d2 in range(2):
                    for i in range(NIC):
                        e.matmul(out_ps[:, po + d2 * 512:po + d2 * 512 + 512],
                                 hqTt_sb[:, tc % 3, i, :],
                                 wdT_sb[:, i, d2 * 512:(d2 + 1) * 512],
                                 start=(i == 0), stop=False)
                    for hp in range(HP):
                        r = e.matmul(out_ps[:, po + d2 * 512:po + d2 * 512 + 512],
                                     oTt_sb[:, tc % 3, hp, :],
                                     wo_sb[:, hp, d2 * 512:(d2 + 1) * 512],
                                     start=False, stop=(hp == HP - 1))
                return r
            ev_do[tc] = sch.add(
                "tensor", dofn,
                deps=[ev_x[tc], ev_ox[tc], ev_w, ev_wo,
                      ev_out[tc - 3] if tc >= 3 else None])

            ev_out[tc] = sch.add(
                "vector",
                lambda e, tc=tc: e.tensor_add(
                    out_sb[:, tc % 3], out_ps[:, (tc % 3) * 1024:(tc % 3) * 1024 + 1024],
                    xq_sb[:, tc % 3]),
                deps=[ev_do[tc], ev_ost[tc - 3] if tc >= 3 else None])
            ev_ost[tc] = sch.dma(
                "sync",
                lambda e, tc=tc: e.dma_start(out=out[tc * 128:(tc + 1) * 128, :],
                                             in_=out_sb[:, tc % 3]),
                deps=[ev_out[tc]])
        sch.emit(blk)


_CACHED = {}


def kernel(L0_attn_x, L0_attn_w_q, L0_attn_w_k, L0_attn_w_v, L0_attn_w_o,
           L0_ffn_w_gate, L0_ffn_w_up, L0_ffn_w_down):
    import ml_dtypes
    x = np.ascontiguousarray(np.asarray(L0_attn_x, dtype=np.float32))
    ws = dict(
        wq=np.ascontiguousarray(np.asarray(L0_attn_w_q, np.float32)),
        wk=np.ascontiguousarray(np.asarray(L0_attn_w_k, np.float32)),
        wv=np.ascontiguousarray(np.asarray(L0_attn_w_v, np.float32)),
        wo=np.ascontiguousarray(np.asarray(L0_attn_w_o, np.float32)),
        wg=np.ascontiguousarray(np.asarray(L0_ffn_w_gate, np.float32)),
        wu=np.ascontiguousarray(np.asarray(L0_ffn_w_up, np.float32)),
        wd=np.ascontiguousarray(np.asarray(L0_ffn_w_down, np.float32)),
    )
    ident = np.eye(128, dtype=ml_dtypes.bfloat16)
    if "nc" not in _CACHED:
        _CACHED["nc"] = build_program()
    nc = _CACHED["nc"]
    in_maps = []
    for c in range(8):
        b, qs = c // 4, (c % 4) * Q
        in_maps.append(dict(xb=x[b], xq=x[b][qs:qs + Q], identb=ident, **ws))
    res = run_bass_kernel_spmd(nc, in_maps, core_ids=list(range(8)))
    out = np.empty((2, S, D), dtype=np.float32)
    for c in range(8):
        b, qs = c // 4, (c % 4) * Q
        out[b, qs:qs + Q] = res.results[c]["out"]
    return out
